# version 06
# baseline (speedup 1.0000x reference)
import numpy as np
import ml_dtypes
from contextlib import ExitStack

import concourse.bass as bass
import concourse.mybir as mybir
from concourse.bass_utils import run_bass_kernel_spmd

F32 = mybir.dt.float32
BF16 = mybir.dt.bfloat16
ALU = mybir.AluOpType
AF = mybir.ActivationFunctionType
AX = mybir.AxisListType

DBG_BAR = False
PHA_DB = False
KTM_DB = True
D = 2048
T = 1024
NCH = 8
NFC = 16
NH = 16
DH = 128
NSEL = 320
NJC = 3
JSZ = [128, 128, 64]
HID_F = 8192
HID_A = 5461
NHA = 43
HSZ = [128] * 42 + [85]
EPS = 1e-5
RG2 = [[0, 1], [2, 3], [4, 5], [6, 7]]

BF = ml_dtypes.bfloat16


class K:
    QSEM = {"S": "dS", "C": "dC", "G": "dG"}
    ESEM = {"T": "eT", "V": "eV", "C": "eC", "G": "eG", "S": "eS"}

    def __init__(self, nc, ctx):
        self.nc = nc
        self.prog = {e: [] for e in "TVCGS"}
        self.sems = {}
        self.cnt = {}
        for s in ("dS", "dC", "dG", "eT", "eV", "eC", "eG", "eS"):
            self.sems[s] = ctx.enter_context(nc.semaphore(s))
            self.cnt[s] = 0
        self.seen = {}

    def wait(self, E, ev):
        if ev is None:
            return
        sem, val = ev
        if val <= 0:
            return
        key = (E, sem)
        if self.seen.get(key, 0) >= val:
            return
        self.seen[key] = val
        h = self.sems[sem]
        self.prog[E].append(lambda eng, h=h, val=val: eng.wait_ge(h, val))

    def waits(self, E, evs):
        for ev in evs:
            self.wait(E, ev)

    def op(self, E, fn, tag=False, waits=()):
        for ev in waits:
            self.wait(E, ev)
        if tag:
            sem = self.ESEM[E]
            self.cnt[sem] += 1
            val = self.cnt[sem]
            h = self.sems[sem]

            def run(eng, fn=fn, h=h):
                fn(eng).then_inc(h, 1)

            self.prog[E].append(run)
            return (sem, val)
        self.prog[E].append(fn)
        return None

    def dma(self, Q, out, in_, waits=()):
        sem = self.QSEM[Q]
        for ev in waits:
            self.wait(Q, ev)
        self.cnt[sem] += 16
        val = self.cnt[sem]
        h = self.sems[sem]

        def run(eng, out=out, in_=in_, h=h):
            eng.dma_start(out=out, in_=in_).then_inc(h, 16)

        self.prog[Q].append(run)
        return (sem, val)

    def now(self, sem):
        return (sem, self.cnt[sem])

    def barrier(self, engines="TVCGS"):
        evs = [(s, self.cnt[s]) for s in self.cnt]
        for E in engines:
            for ev in evs:
                self.wait(E, ev)

    def replay(self, block):
        for E, meth in (("T", block.tensor), ("V", block.vector),
                        ("C", block.scalar), ("G", block.gpsimd),
                        ("S", block.sync)):
            ops = self.prog[E]

            def body(eng, ops=ops):
                for f in ops:
                    f(eng)

            meth(body)


class PsumBank:
    def __init__(self, ap):
        self.ap = ap
        self.free_ev = None


def gamma_vec():
    return 1.0 - 2.0 ** (-5.0 - np.arange(16, dtype=np.float64))


def build_program(parts=("ffn1", "attn", "ffn3")):
    nc = bass.Bass("TRN2", target_bir_lowering=False)
    ctx = ExitStack()

    di = lambda n, sh, dt=F32: nc.dram_tensor(n, sh, dt, kind="ExternalInput").ap()
    x_in = di("x_in", [T, D])
    wq = di("wq", [D, D]); wk = di("wk", [D, D]); wv = di("wv", [D, D])
    wg = di("wg", [D, D]); wo = di("wo", [D, D])
    aw1 = di("aw1", [D, HID_A]); aw2 = di("aw2", [D, HID_A]); aw3 = di("aw3", [HID_A, D])
    W1 = di("W1", [D, HID_F]); W2 = di("W2", [HID_F, D])
    b1_i = di("b1_i", [128, 64])
    b2_i = di("b2_i", [1, D])
    wr_b_i = di("wr_b_i", [128, D])
    br_i = di("br_i", [128, 1])
    n1w_i = di("n1w_i", [128, 16]); n1b_i = di("n1b_i", [128, 16])
    n2w_i = di("n2w_i", [128, 16]); n2b_i = di("n2b_i", [128, 16])
    mq_i = di("mq_i", [NH, 128, T])
    mk_i = di("mk_i", [NH, 128, T])
    flag_i = di("flag_i", [128, 2])
    o_out = nc.dram_tensor("o", [T, D], F32, kind="ExternalOutput").ap()
    o_dbg = nc.dram_tensor("o_dbg", [128, 96], F32, kind="ExternalOutput").ap()



    dint = lambda n, sh, dt=F32: nc.dram_tensor(n, sh, dt, kind="Internal").ap()
    ccw_in = [dint(f"ccw_in{i}", [128, 8]) for i in range(2)]
    ccw_out = [dint(f"ccw_out{i}", [2, 128, 8]) for i in range(2)]
    ccs_in = dint("ccs_in", [128, NH * DH], BF16)
    ccs_out = dint("ccs_out", [2, 128, NH * DH], BF16)
    x_page = dint("x_page", [128, NCH, D])
    zn_page = dint("zn_page", [128, NCH, D], BF16)
    g_page = dint("g_page", [128, NCH, D], BF16)

    ident_bf = nc.inline_tensor(np.eye(128, dtype=BF), name="ident_bf").ap()
    ident_f = nc.inline_tensor(np.eye(128, dtype=np.float32), name="ident_f").ap()
    tri_bf = nc.inline_tensor(np.triu(np.ones((128, 128), np.float32)).astype(BF),
                              name="tri_bf").ap()
    lt_bf = nc.inline_tensor(np.triu(np.ones((128, 128), np.float32), 1).astype(BF),
                             name="lt_bf").ap()
    iota320_c = nc.inline_tensor(
        np.broadcast_to(np.arange(NSEL, dtype=np.float32), (128, NSEL)).copy(),
        name="iota320").ap()
    ones_row_c = nc.inline_tensor(np.ones((1, 128), np.float32), name="ones_row").ap()
    ones_col_c = nc.inline_tensor(np.ones((128, 1), BF), name="ones_col").ap()

    sb = lambda n, sh, dt=F32: ctx.enter_context(nc.sbuf_tensor(n, sh, dt))
    ps = lambda n, sh, dt=F32: ctx.enter_context(nc.psum_tensor(n, sh, dt))

    with ctx:
        k = K(nc, ctx)

        wr_b = sb("wr_b", [128, D])
        b2b = sb("b2b", [128, D])
        iota320 = sb("iota320s", [128, NSEL])
        ident_b = sb("ident_b", [128, 128], BF16)
        ident_s = sb("ident_s", [128, 128])
        tri_s = sb("tri_s", [128, 128], BF16)
        lt_s = sb("lt_s", [128, 128], BF16)
        ones_r = sb("ones_r", [1, 128])
        ones_cb = sb("ones_cb", [128, 1], BF16)
        n1w = sb("n1w", [128, 16]); n1b = sb("n1b", [128, 16])
        n2w = sb("n2w", [128, 16]); n2b = sb("n2b", [128, 16])
        b1c = sb("b1c", [128, 64])
        br_c = sb("br_c", [128, 1])
        flag_c = sb("flag_c", [128, 2])
        eps_c = sb("eps_c", [128, 1])
        swr_c = sb("swr_c", [128, 1])

        st6 = sb("st6", [128, 4, 6])
        mu8 = sb("mu8", [128, NCH]); var8 = sb("var8", [128, NCH])
        rsig8 = sb("rsig8", [128, NCH]); nmr8 = sb("nmr8", [128, NCH])
        mu2 = sb("mu2", [128, NCH]); rsig2 = sb("rsig2", [128, NCH])
        nmr2 = sb("nmr2", [128, NCH])
        gnA = sb("gnA", [128, NH]); gnB = sb("gnB", [128, NH])
        gnS = sb("gnS", [128, NH]); gnQ = sb("gnQ", [128, NH])
        mcol = sb("mcol", [128, 8]); vcol = sb("vcol", [128, 8])
        msq = sb("msq", [128, 8])
        c1 = sb("c1", [128, 1]); c2 = sb("c2", [128, 1]); c3 = sb("c3", [128, 1])
        r08 = sb("r08", [128, NCH])
        w_own = sb("w_own", [128, NCH])
        ge8 = sb("ge8", [128, NCH]); sel8 = sb("sel8", [128, NCH])
        sel_bf = sb("sel_bf", [128, NCH], BF16)
        hnsr8 = sb("hnsr8", [128, NCH]); nmhr8 = sb("nmhr8", [128, NCH])
        ws8 = sb("ws8", [128, NCH])
        t8a = sb("t8a", [128, NCH]); t8b = sb("t8b", [128, NCH])
        scrpt = sb("scrpt", [128, NSEL], BF16)
        crow = sb("crow", [1, 8]); coff = sb("coff", [1, 8])
        pfx = sb("pfx", [128, NCH])

        p512 = [PsumBank(ps(f"p512_{i}", [128, 512])) for i in range(6)]
        pbf = [PsumBank(ps(f"pbf_{i}", [128, 512], BF16)) for i in range(2)]

        KB = 1024
        ARENA = nc.sbuf_bytes_remaining - 10 * KB
        assert ARENA >= 179456, f"arena too small: {ARENA}"
        arena_h = ctx.enter_context(
            nc.sbuf_tensor("arena", [128, ARENA], mybir.dt.uint8))
        try:
            abase = nc.lookup_mloc(arena_h).addr
        except Exception:
            abase = nc.lookup_mloc(arena_h.tensor).addr

        def at(name, off, shape, dt=F32):
            return nc.alloc_sbuf_tensor_at(name, shape, dt,
                                           offset=abase + off).ap()

        xs_t = at("xs", 0, [128, NCH, D], F32)

        def ffn_bufs(idx):
            F = 64 * KB
            return dict(
                pt=at(f"pt{idx}", F, [128, NCH, NSEL], BF16),
                ptl=at(f"ptl{idx}", F + 5120, [128, NJC * NCH, 128], BF16),
                xgT=at(f"xgT{idx}", F + 11264, [128, NFC, NSEL], BF16),
                vals=at(f"vals{idx}", F + 21504, [128, NJC, D], BF16),
                wb1=[at(f"wb1_{idx}_{i}", F + 33792 + i * 8192,
                        [128, NFC, 256], BF16) for i in range(2)],
                wb2=[at(f"wb2_{idx}_{i}", F + 50176 + i * 4096,
                        [128, 4, 512], BF16) for i in range(3)],
                cs512=at(f"cs{idx}", F + 62464, [128, 512], F32),
                wall=at(f"wall{idx}", F + 64512, [1, 2 * T], F32),
                wbc=at(f"wbc{idx}", F + 72704, [128, 2 * T], F32),
                lnx=at(f"lnx{idx}", F + 80896, [128, NCH, D], BF16),
                hfT=at(f"hfT{idx}", F + 72704, [128, 64, NSEL], BF16),
            )

        ctx_init = ExitStack()
        b2row = ctx_init.enter_context(nc.sbuf_tensor("b2row", [1, D], F32))

        ev_x = k.dma("S", xs_t, x_in.rearrange("(c p) f -> p c f", p=128))
        ev_c = []
        for dst, src in ((wr_b, wr_b_i), (iota320, iota320_c), (ident_s, ident_f),
                         (n1w, n1w_i), (n1b, n1b_i), (n2w, n2w_i), (n2b, n2b_i),
                         (b1c, b1_i), (br_c, br_i), (flag_c, flag_i),
                         (ones_r, ones_row_c), (b2row, b2_i)):
            ev_c.append(k.dma("C", dst[:], src))
        for dst, src in ((ident_b, ident_bf), (tri_s, tri_bf), (lt_s, lt_bf),
                         (ones_cb, ones_col_c)):
            ev_c.append(k.dma("C", dst[:], src))
        ev_eps = k.op("G", lambda g: g.memset(eps_c[:], EPS), tag=True)
        k.wait("C", ev_eps)

        k.waits("V", ev_c[:1])
        ev = k.op("V", lambda v: v.tensor_reduce(swr_c[0:1, :], wr_b[0:1, :], AX.X,
                                                 ALU.add), tag=True)
        k.waits("T", [ev, ev_c[10]])
        evt = k.op("T", lambda t: t.matmul(p512[0].ap[:, 0:1], ones_r[:],
                                           swr_c[0:1, 0:1], start=True, stop=True),
                   tag=True)
        k.wait("V", evt)
        ev_swr = k.op("V", lambda v: v.tensor_copy(swr_c[:], p512[0].ap[:, 0:1]),
                      tag=True)
        ev_b2 = []
        for fg in range(4):
            k.waits("T", [ev_c[11], ev_swr])
            evt = k.op("T", lambda t, fg=fg: t.matmul(
                p512[1 + (fg % 2)].ap[:], ones_r[:], b2row[0:1, 512*fg:512*(fg+1)],
                start=True, stop=True), tag=True)
            k.wait("V", evt)
            ev_b2.append(k.op("V", lambda v, fg=fg: v.tensor_copy(
                b2b[:, 512*fg:512*(fg+1)], p512[1 + (fg % 2)].ap[:]), tag=True))
        p512[0].free_ev = ev_swr
        p512[1].free_ev = ev_b2[2]
        p512[2].free_ev = ev_b2[3]
        ctx_init.close()
        k.barrier()

        def ln_stats(get_chunk, ev_ready, mu_t, rsig_t, nmr_t,
                     extra_router=False, router_scr=None):
            ev_r0 = []
            for c in range(NCH):
                if callable(ev_ready):
                    k.waits("V", ev_ready(c))
                else:
                    k.waits("V", ev_ready if isinstance(ev_ready, list)
                            else [ev_ready])
                xc = get_chunk(c)
                ev = None
                for g in range(4):
                    ev = k.op("V", lambda v, xc=xc, g=g: v.bn_stats(
                        st6[:, g, :], xc[:, 512*g:512*(g+1)]),
                        tag=(g == 3))
                k.wait("V", ev)
                flat = st6.ap().rearrange("p g f -> p (g f)")
                ev = None
                for i in range(8):
                    k.op("V", lambda v, i=i, flat=flat: v.tensor_copy(
                        mcol[:, i:i+1], flat[:, 3*i+1:3*i+2]))
                    ev = k.op("V", lambda v, i=i, flat=flat: v.tensor_copy(
                        vcol[:, i:i+1], flat[:, 3*i+2:3*i+3]), tag=(i == 7))
                k.wait("V", ev)
                ev = k.op("V", lambda v: v.tensor_tensor(msq[:], mcol[:], mcol[:],
                                                         ALU.mult), tag=True)
                k.wait("V", ev)
                k.op("V", lambda v, c=c: v.tensor_reduce(mu_t[:, c:c+1], mcol[:],
                                                         AX.X, ALU.add))
                k.op("V", lambda v: v.tensor_reduce(c1[:], vcol[:], AX.X, ALU.add))
                ev = k.op("V", lambda v: v.tensor_reduce(c2[:], msq[:], AX.X,
                                                         ALU.add), tag=True)
                k.wait("V", ev)
                k.op("V", lambda v, c=c: v.tensor_scalar(
                    mu_t[:, c:c+1], mu_t[:, c:c+1], 0.125, None, ALU.mult))
                k.op("V", lambda v: v.tensor_scalar(c1[:], c1[:], 1.0/2048.0, None,
                                                    ALU.mult))
                ev = k.op("V", lambda v: v.tensor_scalar(c2[:], c2[:], 0.125, None,
                                                         ALU.mult), tag=True)
                k.wait("V", ev)
                ev = k.op("V", lambda v, c=c: v.tensor_tensor(
                    c3[:], mu_t[:, c:c+1], mu_t[:, c:c+1], ALU.mult), tag=True)
                k.wait("V", ev)
                ev = k.op("V", lambda v: v.tensor_tensor(c2[:], c2[:], c3[:],
                                                         ALU.subtract), tag=True)
                k.wait("V", ev)
                ev = k.op("V", lambda v, c=c: v.tensor_tensor(
                    var8[:, c:c+1], c1[:], c2[:], ALU.add), tag=True)
                k.wait("V", ev)
                if extra_router:
                    ev_r0.append(k.op("V", lambda v, xc=xc, c=c:
                                      v.scalar_tensor_tensor(
                        router_scr[:], xc[:], 1.0, wr_b[:],
                        ALU.bypass, ALU.mult, accum_out=r08[:, c:c+1]),
                        tag=True))
            ev = k.op("V", lambda v: v.tensor_copy(t8a[:], var8[:]), tag=True)
            k.wait("C", ev)
            ev = k.op("C", lambda s: s.activation(t8b[:], var8[:], AF.Sqrt,
                                                  bias=eps_c[:], scale=1.0), tag=True)
            k.wait("V", ev)
            ev = k.op("V", lambda v: v.reciprocal(rsig_t[:], t8b[:]), tag=True)
            k.wait("V", ev)
            ev = k.op("V", lambda v: v.tensor_scalar(nmr_t[:], mu_t[:], -1.0, None,
                                                     ALU.mult), tag=True)
            k.wait("V", ev)
            ev = k.op("V", lambda v: v.tensor_tensor(nmr_t[:], nmr_t[:], rsig_t[:],
                                                     ALU.mult), tag=True)
            return ev, ev_r0

        def transpose_tile(src_ap, dst_ap, pb, waits=(), dtype_bf=True,
                           fuse=None):
            idt = ident_b if dtype_bf else ident_s
            w = list(waits)
            if pb.free_ev is not None:
                w.append(pb.free_ev)
            n = src_ap.shape[-1]
            pslice = pb.ap[:, 0:n] if dtype_bf else pb.ap[:, 0:n]
            evt = k.op("T", lambda t: t.transpose(pslice, src_ap, idt[:]),
                       tag=True, waits=w)
            k.wait("V", evt)
            if fuse is None:
                ev = k.op("V", lambda v: v.tensor_copy(dst_ap, pslice), tag=True)
            else:
                ev = k.op("V", lambda v: fuse(v, dst_ap, pslice), tag=True)
            pb.free_ev = ev
            return ev

        def emit_ffn(idx, x_view, ev_x_ready, pre=None):
            B = ffn_bufs(idx)
            pt_sb, ptl_sb, xgT, vals = B["pt"], B["ptl"], B["xgT"], B["vals"]
            wb1, wb2, cs512 = B["wb1"], B["wb2"], B["cs512"]
            wall, wbc, lnx_sb, hfT = B["wall"], B["wbc"], B["lnx"], B["hfT"]
            scrF = at(f"scrF{idx}", 64 * KB + 80896, [128, D], F32)
            cin, cout = ccw_in[idx], ccw_out[idx]
            W1r = W1.rearrange("(fc p) h -> p fc h", p=128)

            def w1_dma(bi, hb, waits=()):
                evs = list(waits)
                ev = None
                for q in range(4):
                    ev = k.dma("G", wb1[bi][:, 4*q:4*(q+1), :],
                               W1r[:, 4*q:4*(q+1), 256*hb:256*(hb+1)],
                               waits=(evs if q == 0 else ()))
                return ev
            ev_wb = [w1_dma(0, 0), w1_dma(1, 1)]
            if pre is None:
                ev_stats, ev_r0 = ln_stats(lambda c: x_view[:, c, :], ev_x_ready,
                                           mu8, rsig8, nmr8, extra_router=True,
                                           router_scr=scrF)
            else:
                ev_stats, ev_r0 = pre
            ev = k.op("V", lambda v: v.tensor_scalar(t8a[:], mu8[:], swr_c[:], None,
                                                     ALU.mult),
                      tag=True, waits=[ev_stats, ev_swr] + ev_r0)
            k.wait("V", ev)
            ev = k.op("V", lambda v: v.tensor_tensor(t8b[:], r08[:], t8a[:],
                                                     ALU.subtract), tag=True)
            k.wait("V", ev)
            ev = k.op("V", lambda v: v.tensor_tensor(t8b[:], t8b[:], rsig8[:],
                                                     ALU.mult), tag=True)
            k.wait("V", ev)
            ev_w = k.op("V", lambda v: v.tensor_scalar(w_own[:], t8b[:], br_c[:],
                                                       None, ALU.add), tag=True)
            if DBG_BAR:
                k.barrier()
            ev_d1 = k.dma("S", cin, w_own[:], waits=[ev_w])
            ev_cc = k.op("G", lambda g: g.collective_compute(
                "AllGather", ALU.bypass, replica_groups=RG2, ins=[cin],
                outs=[cout]), tag=True, waits=[ev_d1])
            ev_d2 = k.dma("S", wall[:],
                          cout.rearrange("a p c -> (a p c)")[None, :],
                          waits=[ev_cc])
            ctxw_dead = [ev_d2]
            ev_bc = []
            for q in range(4):
                pbq = p512[q % 2]
                wl = [ev_d2]
                if pbq.free_ev is not None:
                    wl.append(pbq.free_ev)
                evt = k.op("T", lambda t, q=q, pbq=pbq: t.matmul(
                    pbq.ap[:], ones_r[:], wall[0:1, 512*q:512*(q+1)],
                    start=True, stop=True), tag=True, waits=wl)
                k.wait("V", evt)
                ev = k.op("V", lambda v, q=q, pbq=pbq: v.tensor_copy(
                    wbc[:, 512*q:512*(q+1)], pbq.ap[:]), tag=True)
                pbq.free_ev = ev
                ev_bc.append(ev)
            k.waits("V", ev_bc)
            ev_ge = []
            for c in range(NCH):
                ev_ge.append(k.op("V", lambda v, c=c: v.tensor_scalar(
                    scrF[:], wbc[:], w_own[:, c:c+1], 0.0, ALU.is_ge, ALU.add,
                    accum_out=ge8[:, c:c+1]), tag=True))
            k.waits("V", ev_ge)
            ev = k.op("V", lambda v: v.tensor_scalar(sel8[:], ge8[:], 511.5, None,
                                                     ALU.is_le), tag=True)
            k.wait("V", ev)
            ev_self = k.op("V", lambda v: v.tensor_copy(sel_bf[:], sel8[:]),
                           tag=True)
            ev = k.op("V", lambda v: v.tensor_scalar(hnsr8[:], sel8[:], -0.5, 0.5,
                                                     ALU.mult, ALU.add), tag=True)
            k.wait("V", ev)
            ev = k.op("V", lambda v: v.tensor_tensor(hnsr8[:], hnsr8[:], rsig8[:],
                                                     ALU.mult), tag=True)
            k.wait("V", ev)
            ev = k.op("V", lambda v: v.tensor_scalar(nmhr8[:], mu8[:], -1.0, None,
                                                     ALU.mult), tag=True)
            k.wait("V", ev)
            ev = k.op("V", lambda v: v.tensor_tensor(nmhr8[:], nmhr8[:], hnsr8[:],
                                                     ALU.mult), tag=True)
            k.wait("V", ev)
            ev = k.op("V", lambda v: v.tensor_scalar(t8a[:], w_own[:], 0.5, None,
                                                     ALU.mult), tag=True)
            k.wait("V", ev)
            ev_ws = k.op("V", lambda v: v.tensor_tensor(ws8[:], t8a[:], sel8[:],
                                                        ALU.mult), tag=True)
            ctxw_dead.append(ev_ws)
            pbA, pbB = p512[2], p512[3]
            wl = [ev_self]
            for b_ in (pbA, pbB):
                if b_.free_ev is not None:
                    wl.append(b_.free_ev)
            evt1 = k.op("T", lambda t: t.matmul(pbA.ap[:, 0:8], lt_s[:], sel_bf[:],
                                                start=True, stop=True),
                        tag=True, waits=wl)
            evt2 = k.op("T", lambda t: t.matmul(pbB.ap[0:1, 0:8], ones_cb[:],
                                                sel_bf[:], start=True, stop=True),
                        tag=True)
            k.waits("V", [evt1, evt2])
            ev = k.op("V", lambda v: v.tensor_copy(pfx[:], pbA.ap[:, 0:8]),
                      tag=True)
            pbA.free_ev = ev
            ev = k.op("V", lambda v: v.tensor_copy(crow[:], pbB.ap[0:1, 0:8]),
                      tag=True)
            pbB.free_ev = ev
            k.wait("V", ev)
            ev = k.op("V", lambda v: v.tensor_tensor_scan(
                coff[0:1, :], crow[0:1, :], crow[0:1, :], 0.0, ALU.add,
                ALU.bypass), tag=True)
            k.wait("V", ev)
            k.op("V", lambda v: v.tensor_copy(crow[0:1, 1:8], coff[0:1, 0:7]))
            ev = k.op("V", lambda v: v.memset(crow[0:1, 0:1], 0.0), tag=True)
            wl = [ev]
            if pbA.free_ev is not None:
                wl.append(pbA.free_ev)
            evt = k.op("T", lambda t: t.matmul(pbA.ap[:, 0:8], ones_r[:],
                                               crow[0:1, :], start=True, stop=True),
                       tag=True, waits=wl)
            k.wait("V", evt)
            ev = k.op("V", lambda v: v.tensor_tensor(pfx[:], pfx[:],
                                                     pbA.ap[:, 0:8], ALU.add),
                      tag=True)
            pbA.free_ev = ev
            k.wait("V", ev)
            ev = k.op("V", lambda v: v.tensor_tensor(pfx[:], pfx[:], sel8[:],
                                                     ALU.mult), tag=True)
            k.wait("V", ev)
            ev = k.op("V", lambda v: v.tensor_tensor(pfx[:], pfx[:], sel8[:],
                                                     ALU.add), tag=True)
            k.wait("V", ev)
            ev_pfx = k.op("V", lambda v: v.tensor_scalar(pfx[:], pfx[:], 1.0, None,
                                                         ALU.subtract), tag=True)
            if DBG_BAR:
                k.barrier()
            k.wait("V", ev_pfx)
            ev_pt = []
            for c in range(NCH):
                ev = k.op("V", lambda v, c=c: v.tensor_scalar(
                    pt_sb[:, c, :], iota320[:], pfx[:, c:c+1], None, ALU.is_equal),
                    tag=True, waits=[ev_pfx])
                ev_pt.append(ev)
            ev_ptl = {}
            for c in range(NCH):
                ev = k.op("V", lambda v, c=c: v.tensor_scalar(
                    scrpt[:], pt_sb[:, c, :], ws8[:, c:c+1], None,
                    ALU.mult), tag=True, waits=[ev_pt[c], ev_ws])
                k.wait("V", ev)
                for jc in range(NJC):
                    pb = pbf[jc % 2]
                    jn = JSZ[jc]
                    wl = [ev]
                    if pb.free_ev is not None:
                        wl.append(pb.free_ev)
                    evt = k.op("T", lambda t, c=c, jc=jc, pb=pb, jn=jn: t.transpose(
                        pb.ap[0:jn, 0:128], scrpt[:, 128*jc:128*jc+jn],
                        ident_b[:]), tag=True, waits=wl)
                    k.wait("V", evt)
                    ev2 = k.op("V", lambda v, c=c, jc=jc, pb=pb, jn=jn:
                               v.tensor_copy(ptl_sb[0:jn, jc*NCH + c, :],
                                             pb.ap[0:jn, 0:128]), tag=True)
                    pb.free_ev = ev2
                    ev_ptl[(jc, c)] = ev2
            ev_lnx = []
            for c in range(NCH):
                ev = k.op("C", lambda s, c=c: s.activation(
                    lnx_sb[:, c, :], x_view[:, c, :], AF.Identity,
                    bias=nmr8[:, c:c+1], scale=rsig8[:, c:c+1]),
                    tag=True, waits=[ev_stats] + ctxw_dead)
                ev_lnx.append(ev)
            for fc in range(NFC):
                pb = p512[fc % 3]
                wl = list(ev_pt) + [ev_lnx[-1]]
                if pb.free_ev is not None:
                    wl.append(pb.free_ev)
                evt = None
                for c in range(NCH):
                    e = k.op("T", lambda t, fc=fc, c=c, pb=pb: t.matmul(
                        pb.ap[:, 0:NSEL], lnx_sb[:, c, 128*fc:128*(fc+1)],
                        pt_sb[:, c, :], start=(c == 0), stop=(c == NCH - 1)),
                        tag=(c == NCH - 1), waits=(wl if c == 0 else ()))
                    if e:
                        evt = e
                k.wait("V", evt)
                ev = k.op("V", lambda v, fc=fc, pb=pb: v.tensor_copy(
                    xgT[:, fc, :], pb.ap[:, 0:NSEL]), tag=True)
                pb.free_ev = ev
            ev_xg = ev
            k.waits("V", [evt])
            if DBG_BAR:
                k.barrier()
            if DBG_BAR:
                k.barrier()
            W1r = W1.rearrange("(fc p) h -> p fc h", p=128)

            def w1_dma(bi, hb, waits=()):
                evs = list(waits)
                ev = None
                for q in range(4):
                    ev = k.dma("G", wb1[bi][:, 4*q:4*(q+1), :],
                               W1r[:, 4*q:4*(q+1), 256*hb:256*(hb+1)],
                               waits=(evs if q == 0 else ()))
                return ev
            ev_wb = [None, None]
            last_use = [None, None]
            ev_wb[0] = w1_dma(0, 0)
            ev_wb[1] = w1_dma(1, 1)
            ev_hf = None
            for hb in range(32):
                bi = hb % 2
                buf = wb1[bi]
                evt = None
                for h2 in range(2):
                    hc = 2*hb + h2
                    pb = p512[hc % 3]
                    wl = [ev_wb[bi], ev_xg]
                    if pb.free_ev is not None:
                        wl.append(pb.free_ev)
                    for fc in range(NFC):
                        e = k.op("T", lambda t, fc=fc, pb=pb, h2=h2, buf=buf:
                                 t.matmul(pb.ap[:, 0:NSEL],
                                          buf[:, fc, 128*h2:128*(h2+1)],
                                          xgT[:, fc, :], start=(fc == 0),
                                          stop=(fc == NFC - 1)),
                                 tag=(fc == NFC - 1),
                                 waits=(wl if fc == 0 else ()))
                        if e:
                            evt = e
                    k.wait("C", evt)
                    ev_hf = k.op("C", lambda s, hc=hc, pb=pb: s.activation(
                        hfT[:, hc, :], pb.ap[:, 0:NSEL], AF.Gelu_apprx_tanh,
                        bias=b1c[:, hc:hc+1], scale=1.0), tag=True)
                    pb.free_ev = ev_hf
                last_use[bi] = evt
                if hb + 2 < 32:
                    ev_wb[bi] = w1_dma(bi, hb + 2, waits=[last_use[bi]])
            W2r = W2.rearrange("(hq p) f -> p hq f", p=128)
            ev_w2 = {}
            use_w2 = {}
            def w2_dma(i):
                fg, hq = i // 16, i % 16
                wl = [use_w2[i - 3]] if (i - 3) in use_w2 else []
                ev_w2[i] = k.dma("G", wb2[i % 3][:],
                                 W2r[:, 4*hq:4*(hq+1), 512*fg:512*(fg+1)],
                                 waits=wl)
            for i in range(3):
                w2_dma(i)
            if DBG_BAR:
                k.barrier()
            ev_vals = []
            for fg in range(4):
                pbs = [p512[3], p512[4], p512[5]]
                wl0 = [ev_hf]
                for b_ in pbs:
                    if b_.free_ev is not None:
                        wl0.append(b_.free_ev)
                evt = None
                for hc in range(64):
                    i = fg * 16 + hc // 4
                    h4 = hc % 4
                    buf = wb2[i % 3]
                    wl = [ev_w2[i]] + (wl0 if hc == 0 else [])
                    for jc in range(NJC):
                        jn = JSZ[jc]
                        e = k.op("T", lambda t, hc=hc, jc=jc, buf=buf, h4=h4,
                                 jn=jn, pb=pbs[jc]: t.matmul(
                            pb.ap[0:jn, :], hfT[:, hc, 128*jc:128*jc+jn],
                            buf[:, h4, :], start=(hc == 0), stop=(hc == 63)),
                            tag=((hc == 63 and jc == NJC - 1)
                                 or (h4 == 3 and jc == NJC - 1)),
                            waits=(wl if jc == 0 else ()))
                        if e:
                            evt = e
                    if h4 == 3:
                        use_w2[i] = evt
                        if i + 3 < 64:
                            w2_dma(i + 3)
                k.wait("V", evt)
                for jc in range(NJC):
                    jn = JSZ[jc]
                    ev = k.op("V", lambda v, fg=fg, jc=jc, jn=jn, pb=pbs[jc]:
                              v.tensor_tensor(vals[0:jn, jc, 512*fg:512*(fg+1)],
                                              pb.ap[0:jn, :],
                                              b2b[0:jn, 512*fg:512*(fg+1)],
                                              ALU.add), tag=True)
                    pbs[jc].free_ev = ev
                    ev_vals.append(ev)
            if DBG_BAR:
                k.barrier()
            ev_done = []
            prev3 = {fg: None for fg in range(4)}
            prev_any = [None]
            for c in range(NCH):
                for fg in range(4):
                    pb = p512[(c*4 + fg) % 3]
                    wl = [ev_vals[-1], ev_ptl[(NJC-1, c)]]
                    if pb.free_ev is not None:
                        wl.append(pb.free_ev)
                    evt = None
                    for jc in range(NJC):
                        jn = JSZ[jc]
                        e = k.op("T", lambda t, c=c, fg=fg, jc=jc, jn=jn, pb=pb:
                                 t.matmul(pb.ap[:],
                                          ptl_sb[0:jn, jc*NCH + c, :],
                                          vals[0:jn, jc, 512*fg:512*(fg+1)],
                                          start=(jc == 0), stop=(jc == NJC - 1)),
                                 tag=(jc == NJC - 1),
                                 waits=(wl if jc == 0 else ()))
                        if e:
                            evt = e
                    wl1 = [evt]
                    if prev_any[0] is not None:
                        wl1.append(prev_any[0])
                    ev1 = k.op("C", lambda s, c=c, fg=fg: s.activation(
                        cs512[:],
                        x_view[:, c, 512*fg:512*(fg+1)], AF.Identity,
                        bias=nmhr8[:, c:c+1], scale=hnsr8[:, c:c+1]), tag=True,
                        waits=wl1)
                    k.waits("V", [evt, ev1])
                    ev2 = k.op("V", lambda v, pb=pb: v.tensor_tensor(
                        cs512[:], cs512[:], pb.ap[:], ALU.add), tag=True)
                    pb.free_ev = ev2
                    k.wait("V", ev2)
                    ev3 = k.op("V", lambda v, c=c, fg=fg: v.tensor_tensor(
                        x_view[:, c, 512*fg:512*(fg+1)],
                        x_view[:, c, 512*fg:512*(fg+1)],
                        cs512[:], ALU.add), tag=True)
                    prev_any[0] = ev3
                    ev_done.append(ev3)
            return [ev_done[4*c + 3] for c in range(NCH)]


        def emit_attn(ev_x_ready, pre_s2=None):
            hT = at("hT", 64 * KB, [128, NFC, T], BF16)
            znbf2 = [at(f"znbf{i}", (96 + 8 * i) * KB, [128, D], BF16)
                     for i in range(2)]
            znb5 = at("znb5", 96 * KB, [128, 512], BF16)
            zn22 = [at(f"zn2{i}", (100 + 8 * i) * KB, [128, D], BF16)
                    for i in range(2)]
            v_sb = at("v_sb", 0, [128, NCH, D], BF16)
            ret = at("ret", 32 * KB, [128, NCH, D], BF16)
            qT = at("qT", 104 * KB, [128, NH, T], BF16)
            kT = at("kT", 136 * KB, [128, T], BF16)
            ktm2 = [at(f"ktm{i}", 138 * KB + 256 * i, [128, 128], BF16)
                    for i in range(2)]
            scT = at("scT", 139 * KB, [128, NCH, T], BF16)
            sown = at("sown", 155 * KB, [128, D], BF16)
            sp = at("sp", 159 * KB, [128, D], BF16)
            wpb = [at(f"wpb{i}", (136 + 16 * i) * KB, [128, NFC, 512], BF16)
                   for i in range(2)]
            mqb = [at(f"mqb{i}", (168 + 4 * i) * KB, [128, T], F32)
                   for i in range(2)]
            wkb = at("wkb", 163 * KB, [128, NFC, 128], BF16)
            mkb = [at(f"mkb{i}", (167 + 4 * i) * KB, [128, T], F32)
                   for i in range(2)]
            gsb = at("gsb", 0, [128, NCH, D], BF16)
            wgb = [at("wgb0", 136 * KB, [128, NFC, 512], BF16),
                   at("wgb1", 163 * KB, [128, NFC, 512], BF16)]
            uT = at("uT", 32 * KB, [128, NFC, T], BF16)
            wob = [at(f"wob{i}", (64 + 16 * i) * KB, [128, NFC, 512], BF16)
                   for i in range(2)]
            xblk = at("xblk", 100 * KB, [128, NCH, D], F32)
            zn4 = at("zn4", 64 * KB, [128, D], BF16)
            h2T = at("h2T", 68 * KB, [128, NFC, T], BF16)
            u2 = at("u2", 0, [128, NHA, 512], BF16)
            awb = [at(f"awb{i}", (44 + 4 * i) * KB, [128, NFC, 128], BF16)
                   for i in range(2)]
            aw2b = [at(f"aw2b{i}", (52 + 4 * i) * KB, [128, NFC, 128], BF16)
                    for i in range(2)]
            aTs = at("aTs", 60 * KB, [128, 512], BF16)
            aw3b = [at(f"aw3b{i}", (61 + 1 * i) * KB, [128, 512], BF16)
                    for i in range(2)]
            xbuf = at("xbuf", 164 * KB, [128, D], F32)
            scrA = at("scrA", 168 * KB, [128, D], F32)

            if pre_s2 is None:
                ev_s2, _ = ln_stats(lambda c: xs_t[:, c, :], ev_x_ready,
                                    mu2, rsig2, nmr2, extra_router=False)
            else:
                ev_s2 = pre_s2
            ev_ht = None
            ev_xpage = []
            prevA = {0: [], 1: []}
            prevC = [None]
            for c in range(NCH):
                znbf = znbf2[c % 2 if PHA_DB else 0]
                zn2 = zn22[c % 2 if PHA_DB else 0]
                if prevC[0] is not None:
                    k.wait("V", prevC[0])
                ev1 = k.op("C", lambda s, c=c, znbf=znbf: s.activation(
                    znbf[:], xs_t[:, c, :], AF.Identity,
                    bias=nmr2[:, c:c+1], scale=rsig2[:, c:c+1]),
                    tag=True, waits=[ev_s2] + prevA[c % 2])
                ev_znd = k.dma("S", zn_page[:, c, :], znbf[:], waits=[ev1])
                k.wait("V", ev1)
                ev = None
                for g in range(4):
                    ev = k.op("V", lambda v, g=g: v.bn_stats(
                        st6[:, g, :], znbf[:, 512*g:512*(g+1)]), tag=(g == 3))
                k.wait("V", ev)
                flat = st6.ap().rearrange("p g f -> p (g f)")
                ev = None
                for i in range(8):
                    k.op("V", lambda v, i=i, flat=flat: v.tensor_copy(
                        mcol[:, i:i+1], flat[:, 3*i+1:3*i+2]))
                    ev = k.op("V", lambda v, i=i, flat=flat: v.tensor_copy(
                        vcol[:, i:i+1], flat[:, 3*i+2:3*i+3]), tag=(i == 7))
                k.wait("V", ev)
                ev = k.op("V", lambda v: v.tensor_tensor(msq[:], mcol[:], mcol[:],
                                                         ALU.mult), tag=True)
                k.wait("V", ev)
                k.op("V", lambda v: v.tensor_reduce(c1[:], mcol[:], AX.X, ALU.add))
                k.op("V", lambda v: v.tensor_reduce(c2[:], vcol[:], AX.X, ALU.add))
                ev = k.op("V", lambda v: v.tensor_reduce(c3[:], msq[:], AX.X,
                                                         ALU.add), tag=True)
                k.wait("V", ev)
                k.op("V", lambda v: v.tensor_scalar(c1[:], c1[:], 0.125, None,
                                                    ALU.mult))
                k.op("V", lambda v: v.tensor_scalar(c2[:], c2[:], 1.0/2048.0, None,
                                                    ALU.mult))
                ev = k.op("V", lambda v: v.tensor_scalar(c3[:], c3[:], 0.125, None,
                                                         ALU.mult), tag=True)
                k.wait("V", ev)
                ev = k.op("V", lambda v: v.tensor_tensor(msq[:, 0:1], c1[:], c1[:],
                                                         ALU.mult), tag=True)
                k.wait("V", ev)
                ev = k.op("V", lambda v: v.tensor_tensor(c2[:], c2[:], c3[:],
                                                         ALU.add), tag=True)
                k.wait("V", ev)
                ev = k.op("V", lambda v: v.tensor_tensor(c2[:], c2[:], msq[:, 0:1],
                                                         ALU.subtract), tag=True)
                k.wait("C", ev)
                ev = k.op("C", lambda s: s.activation(c2[:], c2[:], AF.Sqrt,
                                                      bias=eps_c[:], scale=1.0),
                          tag=True)
                k.wait("V", ev)
                ev = k.op("V", lambda v: v.reciprocal(c2[:], c2[:]), tag=True)
                k.wait("V", ev)
                ev = k.op("V", lambda v: v.tensor_tensor(c3[:], c1[:], c2[:],
                                                         ALU.mult), tag=True)
                k.wait("V", ev)
                ev = k.op("V", lambda v: v.tensor_scalar(c3[:], c3[:], -1.0, None,
                                                         ALU.mult), tag=True)
                k.wait("C", ev)
                ev2 = k.op("C", lambda s, zn2=zn2, znbf=znbf: s.activation(
                    zn2[:], znbf[:], AF.Identity, bias=c3[:], scale=c2[:]),
                    tag=True)
                prevC[0] = ev2
                lastT = None
                for fc in range(NFC):
                    pb = pbf[fc % 2]
                    wl = [ev2]
                    if pb.free_ev is not None:
                        wl.append(pb.free_ev)
                    evt = k.op("T", lambda t, fc=fc, pb=pb, zn2=zn2: t.transpose(
                        pb.ap[:, 0:128], zn2[:, 128*fc:128*(fc+1)], ident_b[:]),
                        tag=True, waits=wl)
                    lastT = evt
                    k.wait("V", evt)
                    ev3 = k.op("V", lambda v, c=c, fc=fc, pb=pb: v.tensor_scalar(
                        hT[:, fc, 128*c:128*(c+1)], pb.ap[:, 0:128],
                        n1w[:, fc:fc+1], n1b[:, fc:fc+1], ALU.mult, ALU.add),
                        tag=True)
                    pb.free_ev = ev3
                    ev_ht = ev3
                ev_xpage.append(k.dma("S", x_page[:, c, :], xs_t[:, c, :],
                                      waits=[ev2]))
                prevA[c % 2] = [ev_znd, lastT]
            k.barrier()

            wvr = {0: wv, 1: wg, 2: wo}

            def wblk_dma(which, og, dst, wl):
                src = wvr[which]
                srcr = src.rearrange("(fc p) f -> p fc f", p=128)
                ev = None
                for q in range(4):
                    ev = k.dma("G", dst[:, 4*q:4*(q+1), :],
                               srcr[:, 4*q:4*(q+1), 512*og:512*(og+1)],
                               waits=(wl if q == 0 else ()))
                return ev

            ev_wv = [wblk_dma(0, 0, wpb[0], []), wblk_dma(0, 1, wpb[1], [])]
            last_v = [None, None]
            ev_vdone = None
            for og in range(4):
                bi = og % 2
                evt = None
                for c in range(NCH):
                    pb = p512[c % 3]
                    wl = [ev_wv[bi], ev_ht]
                    if pb.free_ev is not None:
                        wl.append(pb.free_ev)
                    for fc in range(NFC):
                        e = k.op("T", lambda t, c=c, fc=fc, og=og, pb=pb,
                                 buf=wpb[bi]: t.matmul(
                            pb.ap[:], hT[:, fc, 128*c:128*(c+1)],
                            buf[:, fc, :], start=(fc == 0), stop=(fc == NFC - 1)),
                            tag=(fc == NFC - 1), waits=(wl if fc == 0 else ()))
                        if e:
                            evt = e
                    k.wait("V", evt)
                    ev_vdone = k.op("V", lambda v, c=c, og=og, pb=pb:
                                    v.tensor_copy(v_sb[:, c, 512*og:512*(og+1)],
                                                  pb.ap[:]), tag=True)
                    pb.free_ev = ev_vdone
                last_v[bi] = evt
                if og + 2 < 4:
                    ev_wv[bi] = wblk_dma(0, og + 2, wpb[bi], [last_v[bi]])

            wqr = wq.rearrange("(fc p) f -> p fc f", p=128)
            ev_mq = [k.dma("S", mqb[0][:], mq_i[0]), k.dma("S", mqb[1][:], mq_i[1])]

            def wq_dma(bi, hb, wl):
                ev = None
                for q in range(4):
                    ev = k.dma("G", wpb[bi][:, 4*q:4*(q+1), :],
                               wqr[:, 4*q:4*(q+1), 512*hb:512*(hb+1)],
                               waits=(wl if q == 0 else ()))
                return ev
            ev_wq = [wq_dma(0, 0, [last_v[0]]), wq_dma(1, 1, [last_v[1]])]
            last_q = [None, None]
            for hb in range(4):
                bi = hb % 2
                evt = None
                for h4 in range(4):
                    h = 4 * hb + h4
                    for sg in range(2):
                        pb = p512[(h4 * 2 + sg) % 3]
                        wl = [ev_wq[bi], ev_mq[h % 2]]
                        if pb.free_ev is not None:
                            wl.append(pb.free_ev)
                        for fc in range(NFC):
                            e = k.op("T", lambda t, fc=fc, h4=h4, sg=sg, pb=pb,
                                     buf=wpb[bi]: t.matmul(
                                pb.ap[:], buf[:, fc, 128*h4:128*(h4+1)],
                                hT[:, fc, 512*sg:512*(sg+1)],
                                start=(fc == 0), stop=(fc == NFC - 1)),
                                tag=(fc == NFC - 1),
                                waits=(wl if fc == 0 else ()))
                            if e:
                                evt = e
                        k.wait("V", evt)
                        ev = k.op("V", lambda v, h=h, sg=sg, pb=pb: v.tensor_tensor(
                            qT[:, h, 512*sg:512*(sg+1)], pb.ap[:],
                            mqb[h % 2][:, 512*sg:512*(sg+1)], ALU.mult),
                            tag=True)
                        pb.free_ev = ev
                    if h + 2 < NH:
                        ev_mq[h % 2] = k.dma("S", mqb[h % 2][:], mq_i[h + 2],
                                             waits=[ev])
                last_q[bi] = evt
                if hb + 2 < 4:
                    ev_wq[bi] = wq_dma(bi, hb + 2, [last_q[bi]])
            k.barrier()

            wkr = wk.rearrange("(fc p) f -> p fc f", p=128)
            ev_mk = [k.dma("S", mkb[0][:], mk_i[0]), k.dma("S", mkb[1][:], mk_i[1])]
            ev_state = None
            prev_head_done = None
            for h in range(NH):
                wlk = [prev_head_done] if prev_head_done else []
                ev_wk = None
                for q in range(4):
                    ev_wk = k.dma("G", wkb[:, 4*q:4*(q+1), :],
                                  wkr[:, 4*q:4*(q+1), 128*h:128*(h+1)],
                                  waits=(wlk if q == 0 else ()))
                ev_kT = None
                for sg in range(2):
                    pb = p512[sg % 2]
                    wl = [ev_wk, ev_mk[h % 2]]
                    if pb.free_ev is not None:
                        wl.append(pb.free_ev)
                    evt = None
                    for fc in range(NFC):
                        e = k.op("T", lambda t, fc=fc, sg=sg, pb=pb: t.matmul(
                            pb.ap[:], wkb[:, fc, :],
                            hT[:, fc, 512*sg:512*(sg+1)],
                            start=(fc == 0), stop=(fc == NFC - 1)),
                            tag=(fc == NFC - 1), waits=(wl if fc == 0 else ()))
                        if e:
                            evt = e
                    k.wait("V", evt)
                    ev_kT = k.op("V", lambda v, sg=sg, pb=pb, h=h: v.tensor_tensor(
                        kT[:, 512*sg:512*(sg+1)], pb.ap[:],
                        mkb[h % 2][:, 512*sg:512*(sg+1)], ALU.mult), tag=True)
                    pb.free_ev = ev_kT
                if h + 2 < NH:
                    ev_mk[h % 2] = k.dma("S", mkb[h % 2][:], mk_i[h + 2],
                                         waits=[ev_kT])
                pS = p512[2]
                wlS = [ev_kT]
                if pS.free_ev is not None:
                    wlS.append(pS.free_ev)
                ev_ktm_prev = {0: None, 1: None}
                ev_state_last = None
                for c in range(NCH):
                    pb = pbf[c % 2]
                    ktm = ktm2[c % 2 if KTM_DB else 0]
                    wl = [ev_kT]
                    if pb.free_ev is not None:
                        wl.append(pb.free_ev)
                    evt = k.op("T", lambda t, c=c, pb=pb: t.transpose(
                        pb.ap[:, 0:128], kT[:, 128*c:128*(c+1)], ident_b[:]),
                        tag=True, waits=wl)
                    k.wait("V", evt)
                    evk = k.op("V", lambda v, pb=pb, ktm=ktm: v.tensor_copy(
                        ktm[:], pb.ap[:, 0:128]), tag=True,
                        waits=[ev_ktm_prev[c % 2]] if ev_ktm_prev[c % 2] else [])
                    pb.free_ev = evk
                    wl2 = [evk] + (wlS if c == 0 else [])
                    ev_state_last = k.op("T", lambda t, c=c, h=h, pS=pS, ktm=ktm:
                                         t.matmul(
                        pS.ap[:, 0:128], ktm[:], v_sb[:, c, 128*h:128*(h+1)],
                        start=(c == 0), stop=(c == NCH - 1)),
                        tag=True, waits=wl2)
                    ev_ktm_prev[c % 2] = ev_state_last
                ev_ktm_prev = ev_state_last
                k.wait("V", ev_ktm_prev)
                ev_state = k.op("V", lambda v, h=h, pS=pS: v.tensor_copy(
                    sown[:, 128*h:128*(h+1)], pS.ap[:, 0:128]), tag=True)
                pS.free_ev = ev_state
                ev_sc = []
                for tc in range(NCH):
                    s0 = 128 * tc
                    first = True
                    while s0 < T:
                        wdt = min(512, T - s0)
                        pb = p512[3 + (len(ev_sc) % 2)]
                        wl = [ev_kT]
                        if pb.free_ev is not None:
                            wl.append(pb.free_ev)
                        evt = k.op("T", lambda t, tc=tc, s0=s0, wdt=wdt, h=h,
                                   pb=pb: t.matmul(
                            pb.ap[:, 0:wdt], kT[:, 128*tc:128*(tc+1)],
                            qT[:, h, s0:s0+wdt], start=True, stop=True),
                            tag=True, waits=wl)
                        k.wait("V", evt)
                        if first:
                            ev = k.op("V", lambda v, tc=tc, s0=s0, pb=pb:
                                      v.tensor_tensor(
                                scT[:, tc, s0:s0+128], pb.ap[:, 0:128],
                                tri_s[:], ALU.mult), tag=True)
                            if wdt > 128:
                                ev = k.op("V", lambda v, tc=tc, s0=s0, wdt=wdt,
                                          pb=pb: v.tensor_copy(
                                    scT[:, tc, s0+128:s0+wdt],
                                    pb.ap[:, 128:wdt]), tag=True)
                        else:
                            ev = k.op("V", lambda v, tc=tc, s0=s0, wdt=wdt, pb=pb:
                                      v.tensor_copy(scT[:, tc, s0:s0+wdt],
                                                    pb.ap[:, 0:wdt]), tag=True)
                        pb.free_ev = ev
                        ev_sc.append(ev)
                        s0 += wdt
                        first = False
                for sc in range(NCH):
                    pb = p512[5]
                    wl = [ev_sc[-1]]
                    if pb.free_ev is not None:
                        wl.append(pb.free_ev)
                    evt = None
                    for tc in range(sc + 1):
                        e = k.op("T", lambda t, tc=tc, sc=sc, h=h, pb=pb:
                                 t.matmul(
                            pb.ap[:, 0:128], scT[:, tc, 128*sc:128*(sc+1)],
                            v_sb[:, tc, 128*h:128*(h+1)],
                            start=(tc == 0), stop=(tc == sc)),
                            tag=(tc == sc), waits=(wl if tc == 0 else ()))
                        if e:
                            evt = e
                    k.wait("V", evt)
                    ev = k.op("V", lambda v, sc=sc, h=h, pb=pb: v.tensor_copy(
                        ret[:, sc, 128*h:128*(h+1)], pb.ap[:, 0:128]), tag=True)
                    pb.free_ev = ev
                prev_head_done = ev
            ev_d1 = k.dma("S", ccs_in, sown[:], waits=[ev_state])

            ev_wg = [wblk_dma(1, 0, wgb[0], [prev_head_done, ev_d1]),
                     wblk_dma(1, 1, wgb[1], [])]
            last_g = [None, None]
            ev_g = None
            for og in range(4):
                bi = og % 2
                evt = None
                for c in range(NCH):
                    pb = p512[c % 3]
                    wl = [ev_wg[bi]]
                    if pb.free_ev is not None:
                        wl.append(pb.free_ev)
                    for fc in range(NFC):
                        e = k.op("T", lambda t, c=c, fc=fc, pb=pb, buf=wgb[bi]:
                                 t.matmul(
                            pb.ap[:], hT[:, fc, 128*c:128*(c+1)],
                            buf[:, fc, :], start=(fc == 0), stop=(fc == NFC - 1)),
                            tag=(fc == NFC - 1), waits=(wl if fc == 0 else ()))
                        if e:
                            evt = e
                    k.wait("C", evt)
                    ev_g = k.op("C", lambda s, c=c, og=og, pb=pb: s.activation(
                        gsb[:, c, 512*og:512*(og+1)], pb.ap[:], AF.Silu),
                        tag=True)
                    pb.free_ev = ev_g
                last_g[bi] = evt
                if og + 2 < 4:
                    ev_wg[bi] = wblk_dma(1, og + 2, wgb[bi], [last_g[bi]])
            ev_cc = k.op("G", lambda g: g.collective_compute(
                "AllGather", ALU.bypass, replica_groups=RG2, ins=[ccs_in],
                outs=[ccs_out]), tag=True, waits=[ev_d1])
            ev_d2 = k.dma("S", sp[:], ccs_out[0], waits=[ev_cc])
            k.waits("V", [ev_d2])
            ev_sp = k.op("V", lambda v: v.tensor_scalar(
                sp[:], sp[:], flag_c[:, 0:1], None, ALU.mult), tag=True)
            ev_apply = None
            for h in range(NH):
                for sc in range(NCH):
                    pb = p512[(h * NCH + sc) % 3]
                    wl = [ev_sp, prev_head_done]
                    if pb.free_ev is not None:
                        wl.append(pb.free_ev)
                    evt = k.op("T", lambda t, h=h, sc=sc, pb=pb: t.matmul(
                        pb.ap[:, 0:128], qT[:, h, 128*sc:128*(sc+1)],
                        sp[:, 128*h:128*(h+1)], start=True, stop=True),
                        tag=True, waits=wl)
                    k.wait("V", evt)
                    ev_apply = k.op("V", lambda v, h=h, sc=sc, pb=pb:
                                    v.tensor_tensor(
                        ret[:, sc, 128*h:128*(h+1)], ret[:, sc, 128*h:128*(h+1)],
                        pb.ap[:, 0:128], ALU.add), tag=True)
                    pb.free_ev = ev_apply
            k.barrier()
            ev_u = None
            for c in range(NCH):
                ev1 = k.op("C", lambda s, c=c: s.activation(
                    scrA[:], ret[:, c, :], AF.Square), tag=True,
                    waits=[ev_g, ev_apply] + ([ev_u] if ev_u else []))
                k.waits("V", [ev1])
                ev = None
                for h in range(NH):
                    k.op("V", lambda v, c=c, h=h: v.tensor_reduce(
                        gnS[:, h:h+1], ret[:, c, 128*h:128*(h+1)], AX.X, ALU.add))
                    ev = k.op("V", lambda v, h=h: v.tensor_reduce(
                        gnQ[:, h:h+1], scrA[:, 128*h:128*(h+1)], AX.X, ALU.add),
                        tag=(h == NH - 1))
                k.wait("V", ev)
                ev = k.op("V", lambda v: v.tensor_scalar(
                    gnS[:], gnS[:], 1.0/128.0, None, ALU.mult), tag=True)
                k.wait("V", ev)
                ev = k.op("V", lambda v: v.tensor_tensor(gnA[:], gnS[:], gnS[:],
                                                         ALU.mult), tag=True)
                k.wait("V", ev)
                ev = k.op("V", lambda v: v.tensor_scalar(
                    gnQ[:], gnQ[:], 1.0/128.0, None, ALU.mult), tag=True)
                k.wait("V", ev)
                ev = k.op("V", lambda v: v.tensor_tensor(gnQ[:], gnQ[:], gnA[:],
                                                         ALU.subtract), tag=True)
                k.wait("C", ev)
                ev = k.op("C", lambda s: s.activation(gnQ[:], gnQ[:], AF.Sqrt,
                                                      bias=eps_c[:], scale=1.0),
                          tag=True)
                k.wait("V", ev)
                ev = k.op("V", lambda v: v.reciprocal(gnA[:], gnQ[:]), tag=True)
                k.wait("V", ev)
                ev = k.op("V", lambda v: v.tensor_tensor(gnB[:], gnS[:], gnA[:],
                                                         ALU.mult), tag=True)
                k.wait("V", ev)
                evb = k.op("V", lambda v: v.tensor_scalar(gnB[:], gnB[:], -1.0,
                                                          None, ALU.mult),
                           tag=True)
                k.wait("V", evb)
                for h in range(NH):
                    ev = k.op("V", lambda v, c=c, h=h: v.tensor_scalar(
                        scrA[:, 128*h:128*(h+1)], ret[:, c, 128*h:128*(h+1)],
                        gnA[:, h:h+1], gnB[:, h:h+1], ALU.mult, ALU.add),
                        tag=(h == NH - 1))
                k.wait("V", ev)
                ev_u = k.op("V", lambda v, c=c: v.tensor_tensor(
                    gsb[:, c, :], gsb[:, c, :], scrA[:], ALU.mult), tag=True)
            k.barrier()

            ev_uT = None
            for c in range(NCH):
                for fc in range(NFC):
                    pb = pbf[fc % 2]
                    wl = [ev_u]
                    if pb.free_ev is not None:
                        wl.append(pb.free_ev)
                    evt = k.op("T", lambda t, c=c, fc=fc, pb=pb: t.transpose(
                        pb.ap[:, 0:128], gsb[:, c, 128*fc:128*(fc+1)],
                        ident_b[:]), tag=True, waits=wl)
                    k.wait("V", evt)
                    ev_uT = k.op("V", lambda v, c=c, fc=fc, pb=pb: v.tensor_copy(
                        uT[:, fc, 128*c:128*(c+1)], pb.ap[:, 0:128]), tag=True)
                    pb.free_ev = ev_uT
            ev_wo = [wblk_dma(2, 0, wob[0], [ev_uT]), wblk_dma(2, 1, wob[1], [])]
            last_o = [None, None]
            ev_xblk = None
            for og in range(4):
                bi = og % 2
                evt = None
                for c in range(NCH):
                    ev_zn = k.dma("S", znb5[:], zn_page[:, c, 512*og:512*(og+1)],
                                  waits=([ev_xblk] if ev_xblk else []))
                    pb = p512[c % 3]
                    wl = [ev_wo[bi], ev_uT]
                    if pb.free_ev is not None:
                        wl.append(pb.free_ev)
                    for fc in range(NFC):
                        e = k.op("T", lambda t, c=c, fc=fc, pb=pb, buf=wob[bi]:
                                 t.matmul(
                            pb.ap[:], uT[:, fc, 128*c:128*(c+1)],
                            buf[:, fc, :], start=(fc == 0), stop=(fc == NFC - 1)),
                            tag=(fc == NFC - 1), waits=(wl if fc == 0 else ()))
                        if e:
                            evt = e
                    k.waits("V", [evt, ev_zn])
                    ev_xblk = k.op("V", lambda v, c=c, og=og, pb=pb:
                                   v.tensor_tensor(
                        xblk[:, c, 512*og:512*(og+1)], pb.ap[:], znb5[:],
                        ALU.add), tag=True)
                    pb.free_ev = ev_xblk
                last_o[bi] = evt
                if og + 2 < 4:
                    ev_wo[bi] = wblk_dma(2, og + 2, wob[bi], [last_o[bi]])
            k.barrier()

            ev_s4, _ = ln_stats(lambda c: xblk[:, c, :], ev_xblk,
                                mu8, rsig8, nmr8, extra_router=False)
            ev_h2 = None
            lastT4 = None
            for c in range(NCH):
                ev1 = k.op("C", lambda s, c=c: s.activation(
                    zn4[:], xblk[:, c, :], AF.Identity,
                    bias=nmr8[:, c:c+1], scale=rsig8[:, c:c+1]),
                    tag=True, waits=[ev_s4] + ([lastT4] if lastT4 else []))
                for fc in range(NFC):
                    pb = pbf[fc % 2]
                    wl = [ev1]
                    if pb.free_ev is not None:
                        wl.append(pb.free_ev)
                    evt = k.op("T", lambda t, fc=fc, pb=pb: t.transpose(
                        pb.ap[:, 0:128], zn4[:, 128*fc:128*(fc+1)], ident_b[:]),
                        tag=True, waits=wl)
                    lastT4 = evt
                    k.wait("V", evt)
                    ev_h2 = k.op("V", lambda v, c=c, fc=fc, pb=pb: v.tensor_scalar(
                        h2T[:, fc, 128*c:128*(c+1)], pb.ap[:, 0:128],
                        n2w[:, fc:fc+1], n2b[:, fc:fc+1], ALU.mult, ALU.add),
                        tag=True)
                    pb.free_ev = ev_h2
            k.barrier()

            a1r = aw1.rearrange("(fc p) h -> p fc h", p=128)
            a2r = aw2.rearrange("(fc p) h -> p fc h", p=128)
            for sb2 in range(2):
                if sb2 > 0:
                    k.barrier()
                ev_aw = {}
                use_aw = {}

                def aw_dma2(i, ev_aw=ev_aw, use_aw=use_aw):
                    h0 = 128 * i
                    hsz = HSZ[i]
                    wl = [use_aw[i - 2]] if (i - 2) in use_aw else []
                    ev = None
                    for q in range(4):
                        k.dma("G", awb[i % 2][:, 4*q:4*(q+1), 0:hsz],
                              a1r[:, 4*q:4*(q+1), h0:h0+hsz],
                              waits=(wl if q == 0 else ()))
                    for q in range(4):
                        ev = k.dma("G", aw2b[i % 2][:, 4*q:4*(q+1), 0:hsz],
                                   a2r[:, 4*q:4*(q+1), h0:h0+hsz])
                    ev_aw[i] = ev
                for i in range(2):
                    aw_dma2(i)
                ev_u2 = None
                for ha in range(NHA):
                    hsz = HSZ[ha]
                    bi = ha % 2
                    pa, pb2 = p512[0], p512[1]
                    wl = [ev_aw[ha], ev_h2]
                    if ev_u2:
                        wl.append(ev_u2)
                    for b_ in (pa, pb2):
                        if b_.free_ev is not None:
                            wl.append(b_.free_ev)
                    evta = None
                    for fc in range(NFC):
                        e = k.op("T", lambda t, fc=fc, hsz=hsz, sb2=sb2, pa=pa,
                                 bi=bi: t.matmul(
                            pa.ap[0:hsz, :], awb[bi][:, fc, 0:hsz],
                            h2T[:, fc, 512*sb2:512*(sb2+1)],
                            start=(fc == 0), stop=(fc == NFC - 1)),
                            tag=(fc == NFC - 1), waits=(wl if fc == 0 else ()))
                        if e:
                            evta = e
                    evtb = None
                    for fc in range(NFC):
                        e = k.op("T", lambda t, fc=fc, hsz=hsz, sb2=sb2, pb2=pb2,
                                 bi=bi: t.matmul(
                            pb2.ap[0:hsz, :], aw2b[bi][:, fc, 0:hsz],
                            h2T[:, fc, 512*sb2:512*(sb2+1)],
                            start=(fc == 0), stop=(fc == NFC - 1)),
                            tag=(fc == NFC - 1), waits=())
                        if e:
                            evtb = e
                    use_aw[ha] = evtb
                    if ha + 2 < NHA:
                        aw_dma2(ha + 2)
                    k.wait("C", evta)
                    eva = k.op("C", lambda s, hsz=hsz, pa=pa: s.activation(
                        aTs[0:hsz, :], pa.ap[0:hsz, :], AF.Silu), tag=True)
                    pa.free_ev = eva
                    k.waits("V", [eva, evtb])
                    ev_u2 = k.op("V", lambda v, ha=ha, hsz=hsz, pb2=pb2:
                                 v.tensor_tensor(
                        u2[0:hsz, ha, :], aTs[0:hsz, :], pb2.ap[0:hsz, :],
                        ALU.mult), tag=True)
                    pb2.free_ev = ev_u2
                ev_a3 = {}
                use_a3 = {}

                def a3_dma(j, ev_a3=ev_a3, use_a3=use_a3):
                    og, ha = j // NHA, j % NHA
                    hsz = HSZ[ha]
                    wl = [use_a3[j - 2]] if (j - 2) in use_a3 else []
                    ev_a3[j] = k.dma(
                        "G", aw3b[j % 2][0:hsz, :],
                        aw3[128*ha:128*ha+hsz, 512*og:512*(og+1)], waits=wl)
                for j in range(2):
                    a3_dma(j)
                for og in range(4):
                    pbs4 = [p512[2], p512[3], p512[4], p512[5]]
                    wl0 = [ev_u2]
                    for b_ in pbs4:
                        if b_.free_ev is not None:
                            wl0.append(b_.free_ev)
                    evt = None
                    for ha in range(NHA):
                        j = og * NHA + ha
                        hsz = HSZ[ha]
                        for ss in range(4):
                            wl2 = [ev_a3[j]] + (wl0 if ha == 0 else [])
                            e = k.op("T", lambda t, ha=ha, ss=ss, hsz=hsz, j=j,
                                     pb=pbs4[ss]: t.matmul(
                                pb.ap[:], u2[0:hsz, ha, 128*ss:128*(ss+1)],
                                aw3b[j % 2][0:hsz, :],
                                start=(ha == 0), stop=(ha == NHA - 1)),
                                tag=(ss == 3), waits=(wl2 if ss == 0 else ()))
                            if e:
                                evt = e
                        use_a3[j] = evt
                        if j + 2 < 4 * NHA:
                            a3_dma(j + 2)
                    for ss in range(4):
                        c = 4 * sb2 + ss
                        k.wait("V", evt)
                        ev = k.op("V", lambda v, c=c, og=og, pb=pbs4[ss]:
                                  v.tensor_tensor(
                            xblk[:, c, 512*og:512*(og+1)],
                            xblk[:, c, 512*og:512*(og+1)], pb.ap[:], ALU.add),
                            tag=True)
                        pbs4[ss].free_ev = ev
            k.barrier()

            ev_fin = None
            fins = []
            for c in range(NCH):
                ev_xi = k.dma("S", xbuf[:], x_page[:, c, :],
                              waits=([ev_fin] if ev_fin else []))
                k.waits("V", [ev_xi])
                ev_fin = k.op("V", lambda v, c=c: v.tensor_tensor(
                    xs_t[:, c, :], xblk[:, c, :], xbuf[:], ALU.add), tag=True)
                fins.append(ev_fin)
            return fins

        ev = ev_x
        if "ffn1" in parts:
            evs1 = emit_ffn(0, xs_t, ev)
            ev = evs1[-1]
            if "attn" in parts:
                pre_s2, _ = ln_stats(lambda c: xs_t[:, c, :],
                                     lambda c, evs1=evs1: [evs1[c]],
                                     mu2, rsig2, nmr2, extra_router=False)
            k.barrier()
        if "attn" in parts:
            fins = emit_attn(ev, pre_s2=pre_s2 if "ffn1" in parts else None)
            ev = fins[-1]
            if "ffn3" in parts:
                scrF3 = at("scrF3", 172 * KB, [128, D], F32)
                pre3 = ln_stats(lambda c: xs_t[:, c, :],
                                lambda c, fins=fins: [fins[c]],
                                mu8, rsig8, nmr8, extra_router=True,
                                router_scr=scrF3)
            k.barrier()
        if "ffn3" in parts:
            ev = emit_ffn(1, xs_t, ev,
                          pre=(pre3 if "attn" in parts else None))
            ev = ev[-1]
            k.barrier()

        k.barrier()
        dbg = ctx.enter_context(nc.sbuf_tensor("dbg", [128, 96], F32))
        for i, src in enumerate((mu8, rsig8, r08, w_own, ge8, sel8, pfx, ws8)):
            k.op("V", lambda v, i=i, src=src: v.tensor_copy(
                dbg[:, 8*i:8*(i+1)], src[:]))
        ev = k.op("V", lambda v: v.memset(dbg[:, 0:0+1], 0.0), tag=True)
        k.wait("V", ev)
        evd = k.dma("S", o_dbg, dbg[:], waits=[ev])
        evo = k.dma("S", o_out.rearrange("(c p) f -> p c f", p=128), xs_t)
        k.wait("S", evo)
        k.wait("S", evd)

        with nc.Block() as block:
            k.replay(block)
    return nc


def make_in_maps(inputs):
    x = np.ascontiguousarray(inputs["L0_attn_x"], dtype=np.float32)
    gam = gamma_vec()
    dh_scale = DH ** -0.5
    shared = {
        "wq": np.ascontiguousarray(inputs["L0_attn_wq"], np.float32),
        "wk": np.ascontiguousarray(inputs["L0_attn_wk"], np.float32),
        "wv": np.ascontiguousarray(inputs["L0_attn_wv"], np.float32),
        "wg": np.ascontiguousarray(inputs["L0_attn_wg"], np.float32),
        "wo": np.ascontiguousarray(inputs["L0_attn_wo"], np.float32),
        "aw1": np.ascontiguousarray(inputs["L0_attn_w1"], np.float32),
        "aw2": np.ascontiguousarray(inputs["L0_attn_w2"], np.float32),
        "aw3": np.ascontiguousarray(inputs["L0_attn_w3"], np.float32),
        "W1": np.ascontiguousarray(inputs["L0_ffn_W1"], np.float32),
        "W2": np.ascontiguousarray(inputs["L0_ffn_W2"], np.float32),
        "b1_i": np.ascontiguousarray(
            np.asarray(inputs["L0_ffn_b1"], np.float32).reshape(64, 128).T),
        "b2_i": np.asarray(inputs["L0_ffn_b2"], np.float32).reshape(1, D),
        "wr_b_i": np.ascontiguousarray(np.broadcast_to(
            np.asarray(inputs["L0_ffn_W_r"], np.float32).reshape(1, D), (128, D))),
        "br_i": np.ascontiguousarray(np.broadcast_to(
            np.asarray(inputs["L0_ffn_b_r"], np.float32).reshape(1, 1), (128, 1))),
        "n1w_i": np.ascontiguousarray(
            np.asarray(inputs["L0_attn_norm1_w"], np.float32).reshape(16, 128).T),
        "n1b_i": np.ascontiguousarray(
            np.asarray(inputs["L0_attn_norm1_b"], np.float32).reshape(16, 128).T),
        "n2w_i": np.ascontiguousarray(
            np.asarray(inputs["L0_attn_norm2_w"], np.float32).reshape(16, 128).T),
        "n2b_i": np.ascontiguousarray(
            np.asarray(inputs["L0_attn_norm2_b"], np.float32).reshape(16, 128).T),
    }
    in_maps = []
    for c in range(8):
        b, g = c // 2, c % 2
        off = g * T
        s = off + np.arange(T, dtype=np.float64)
        mq = np.empty((NH, 128, T), np.float32)
        mk = np.empty((NH, 128, T), np.float32)
        for h in range(NH):
            mq[h] = np.float32(gam[h] ** s)[None, :]
            mk[h] = np.float32(gam[h] ** (-s) * dh_scale)[None, :]
        flag = np.zeros((128, 2), np.float32)
        if g == 1:
            flag[:, 0] = 1.0
        m = dict(shared)
        m["x_in"] = np.ascontiguousarray(x[b, off:off + T])
        m["mq_i"] = mq
        m["mk_i"] = mk
        m["flag_i"] = flag
        in_maps.append(m)
    return in_maps


_PARTS = ("ffn1", "attn", "ffn3")


def kernel(**inputs):
    in_maps = make_in_maps(inputs)
    nc = build_program(_PARTS)
    res = run_bass_kernel_spmd(nc, in_maps, core_ids=list(range(8)))
    out = np.empty((4, 2048, D), np.float32)
    for c in range(8):
        b, g = c // 2, c % 2
        out[b, g * T:(g + 1) * T] = res.results[c]["o"]
    return out


# version 07
# speedup vs baseline: 1.0135x; 1.0135x over previous
import numpy as np
import ml_dtypes
from contextlib import ExitStack

import concourse.bass as bass
import concourse.mybir as mybir
from concourse.bass_utils import run_bass_kernel_spmd

F32 = mybir.dt.float32
BF16 = mybir.dt.bfloat16
ALU = mybir.AluOpType
AF = mybir.ActivationFunctionType
AX = mybir.AxisListType

DBG_BAR = False
PHA_DB = False
KTM_DB = True
D = 2048
T = 1024
NCH = 8
NFC = 16
NH = 16
DH = 128
NSEL = 320
NJC = 3
JSZ = [128, 128, 64]
HID_F = 8192
HID_A = 5461
NHA = 43
HSZ = [128] * 42 + [85]
EPS = 1e-5
RG2 = [[0, 1], [2, 3], [4, 5], [6, 7]]

BF = ml_dtypes.bfloat16


class K:
    QSEM = {"S": "dS", "C": "dC", "G": "dG"}
    ESEM = {"T": "eT", "V": "eV", "C": "eC", "G": "eG", "S": "eS"}

    def __init__(self, nc, ctx):
        self.nc = nc
        self.prog = {e: [] for e in "TVCGS"}
        self.sems = {}
        self.cnt = {}
        for s in ("dS", "dC", "dG", "eT", "eV", "eC", "eG", "eS"):
            self.sems[s] = ctx.enter_context(nc.semaphore(s))
            self.cnt[s] = 0
        self.seen = {}

    def wait(self, E, ev):
        if ev is None:
            return
        sem, val = ev
        if val <= 0:
            return
        key = (E, sem)
        if self.seen.get(key, 0) >= val:
            return
        self.seen[key] = val
        h = self.sems[sem]
        self.prog[E].append(lambda eng, h=h, val=val: eng.wait_ge(h, val))

    def waits(self, E, evs):
        for ev in evs:
            self.wait(E, ev)

    def op(self, E, fn, tag=False, waits=()):
        for ev in waits:
            self.wait(E, ev)
        if tag:
            sem = self.ESEM[E]
            self.cnt[sem] += 1
            val = self.cnt[sem]
            h = self.sems[sem]

            def run(eng, fn=fn, h=h):
                fn(eng).then_inc(h, 1)

            self.prog[E].append(run)
            return (sem, val)
        self.prog[E].append(fn)
        return None

    def dma(self, Q, out, in_, waits=()):
        sem = self.QSEM[Q]
        for ev in waits:
            self.wait(Q, ev)
        self.cnt[sem] += 16
        val = self.cnt[sem]
        h = self.sems[sem]

        def run(eng, out=out, in_=in_, h=h):
            eng.dma_start(out=out, in_=in_).then_inc(h, 16)

        self.prog[Q].append(run)
        return (sem, val)

    def now(self, sem):
        return (sem, self.cnt[sem])

    def barrier(self, engines="TVCGS"):
        evs = [(s, self.cnt[s]) for s in self.cnt]
        for E in engines:
            for ev in evs:
                self.wait(E, ev)

    def replay(self, block):
        for E, meth in (("T", block.tensor), ("V", block.vector),
                        ("C", block.scalar), ("G", block.gpsimd),
                        ("S", block.sync)):
            ops = self.prog[E]

            def body(eng, ops=ops):
                for f in ops:
                    f(eng)

            meth(body)


class PsumBank:
    def __init__(self, ap):
        self.ap = ap
        self.free_ev = None


def gamma_vec():
    return 1.0 - 2.0 ** (-5.0 - np.arange(16, dtype=np.float64))


def build_program(parts=("ffn1", "attn", "ffn3")):
    nc = bass.Bass("TRN2", target_bir_lowering=False)
    ctx = ExitStack()

    di = lambda n, sh, dt=F32: nc.dram_tensor(n, sh, dt, kind="ExternalInput").ap()
    x_in = di("x_in", [T, D])
    wq = di("wq", [D, D]); wk = di("wk", [D, D]); wv = di("wv", [D, D])
    wg = di("wg", [D, D]); wo = di("wo", [D, D])
    aw1 = di("aw1", [D, HID_A]); aw2 = di("aw2", [D, HID_A]); aw3 = di("aw3", [HID_A, D])
    W1 = di("W1", [D, HID_F]); W2 = di("W2", [HID_F, D])
    b1_i = di("b1_i", [128, 64])
    b2_i = di("b2_i", [1, D])
    wr_b_i = di("wr_b_i", [128, D])
    br_i = di("br_i", [128, 1])
    n1w_i = di("n1w_i", [128, 16]); n1b_i = di("n1b_i", [128, 16])
    n2w_i = di("n2w_i", [128, 16]); n2b_i = di("n2b_i", [128, 16])
    mq_i = di("mq_i", [NH, 128, T])
    mk_i = di("mk_i", [NH, 128, T])
    flag_i = di("flag_i", [128, 2])
    o_out = nc.dram_tensor("o", [T, D], F32, kind="ExternalOutput").ap()
    o_dbg = nc.dram_tensor("o_dbg", [128, 96], F32, kind="ExternalOutput").ap()



    dint = lambda n, sh, dt=F32: nc.dram_tensor(n, sh, dt, kind="Internal").ap()
    ccw_in = [dint(f"ccw_in{i}", [128, 8]) for i in range(2)]
    ccw_out = [dint(f"ccw_out{i}", [2, 128, 8]) for i in range(2)]
    ccs_in = dint("ccs_in", [128, NH * DH], BF16)
    ccs_out = dint("ccs_out", [2, 128, NH * DH], BF16)
    x_page = dint("x_page", [128, NCH, D])
    zn_page = dint("zn_page", [128, NCH, D], BF16)
    g_page = dint("g_page", [128, NCH, D], BF16)
    w1bf = dint("w1bf", [D, HID_F], BF16)
    w2bf = dint("w2bf", [HID_F, D], BF16)
    pc_ev = [None]

    ident_bf = nc.inline_tensor(np.eye(128, dtype=BF), name="ident_bf").ap()
    ident_f = nc.inline_tensor(np.eye(128, dtype=np.float32), name="ident_f").ap()
    tri_bf = nc.inline_tensor(np.triu(np.ones((128, 128), np.float32)).astype(BF),
                              name="tri_bf").ap()
    lt_bf = nc.inline_tensor(np.triu(np.ones((128, 128), np.float32), 1).astype(BF),
                             name="lt_bf").ap()
    iota320_c = nc.inline_tensor(
        np.broadcast_to(np.arange(NSEL, dtype=np.float32), (128, NSEL)).copy(),
        name="iota320").ap()
    ones_row_c = nc.inline_tensor(np.ones((1, 128), np.float32), name="ones_row").ap()
    ones_col_c = nc.inline_tensor(np.ones((128, 1), BF), name="ones_col").ap()

    sb = lambda n, sh, dt=F32: ctx.enter_context(nc.sbuf_tensor(n, sh, dt))
    ps = lambda n, sh, dt=F32: ctx.enter_context(nc.psum_tensor(n, sh, dt))

    with ctx:
        k = K(nc, ctx)

        wr_b = sb("wr_b", [128, D])
        b2b = sb("b2b", [128, D])
        iota320 = sb("iota320s", [128, NSEL])
        ident_b = sb("ident_b", [128, 128], BF16)
        ident_s = sb("ident_s", [128, 128])
        tri_s = sb("tri_s", [128, 128], BF16)
        lt_s = sb("lt_s", [128, 128], BF16)
        ones_r = sb("ones_r", [1, 128])
        ones_cb = sb("ones_cb", [128, 1], BF16)
        n1w = sb("n1w", [128, 16]); n1b = sb("n1b", [128, 16])
        n2w = sb("n2w", [128, 16]); n2b = sb("n2b", [128, 16])
        b1c = sb("b1c", [128, 64])
        br_c = sb("br_c", [128, 1])
        flag_c = sb("flag_c", [128, 2])
        eps_c = sb("eps_c", [128, 1])
        swr_c = sb("swr_c", [128, 1])

        st6 = sb("st6", [128, 4, 6])
        mu8 = sb("mu8", [128, NCH]); var8 = sb("var8", [128, NCH])
        rsig8 = sb("rsig8", [128, NCH]); nmr8 = sb("nmr8", [128, NCH])
        mu2 = sb("mu2", [128, NCH]); rsig2 = sb("rsig2", [128, NCH])
        nmr2 = sb("nmr2", [128, NCH])
        gnA = sb("gnA", [128, NH]); gnB = sb("gnB", [128, NH])
        gnS = sb("gnS", [128, NH]); gnQ = sb("gnQ", [128, NH])
        mcol = sb("mcol", [128, 8]); vcol = sb("vcol", [128, 8])
        msq = sb("msq", [128, 8])
        c1 = sb("c1", [128, 1]); c2 = sb("c2", [128, 1]); c3 = sb("c3", [128, 1])
        r08 = sb("r08", [128, NCH])
        w_own = sb("w_own", [128, NCH])
        ge8 = sb("ge8", [128, NCH]); sel8 = sb("sel8", [128, NCH])
        sel_bf = sb("sel_bf", [128, NCH], BF16)
        hnsr8 = sb("hnsr8", [128, NCH]); nmhr8 = sb("nmhr8", [128, NCH])
        ws8 = sb("ws8", [128, NCH])
        t8a = sb("t8a", [128, NCH]); t8b = sb("t8b", [128, NCH])
        scrpt = sb("scrpt", [128, NSEL], BF16)
        crow = sb("crow", [1, 8]); coff = sb("coff", [1, 8])
        pfx = sb("pfx", [128, NCH])

        p512 = [PsumBank(ps(f"p512_{i}", [128, 512])) for i in range(6)]
        pbf = [PsumBank(ps(f"pbf_{i}", [128, 512], BF16)) for i in range(2)]

        KB = 1024
        ARENA = nc.sbuf_bytes_remaining - 10 * KB
        assert ARENA >= 179456, f"arena too small: {ARENA}"
        arena_h = ctx.enter_context(
            nc.sbuf_tensor("arena", [128, ARENA], mybir.dt.uint8))
        try:
            abase = nc.lookup_mloc(arena_h).addr
        except Exception:
            abase = nc.lookup_mloc(arena_h.tensor).addr

        def at(name, off, shape, dt=F32):
            return nc.alloc_sbuf_tensor_at(name, shape, dt,
                                           offset=abase + off).ap()

        xs_t = at("xs", 0, [128, NCH, D], F32)

        def ffn_bufs(idx):
            F = 64 * KB
            return dict(
                pt=at(f"pt{idx}", F, [128, NCH, NSEL], BF16),
                ptl=at(f"ptl{idx}", F + 5120, [128, NJC * NCH, 128], BF16),
                xgT=at(f"xgT{idx}", F + 11264, [128, NFC, NSEL], BF16),
                vals=at(f"vals{idx}", F + 21504, [128, NJC, D], BF16),
                wb1=[at(f"wb1_{idx}_{i}", F + 33792 + i * 8192,
                        [128, NFC, 256], BF16) for i in range(2)],
                wb2=[at(f"wb2_{idx}_{i}", F + 50176 + i * 4096,
                        [128, 4, 512], BF16) for i in range(3)],
                cs512=at(f"cs{idx}", F + 62464, [128, 512], F32),
                wall=at(f"wall{idx}", F + 64512, [1, 2 * T], F32),
                wbc=at(f"wbc{idx}", F + 72704, [128, 2 * T], F32),
                lnx=at(f"lnx{idx}", F + 80896, [128, NCH, D], BF16),
                hfT=at(f"hfT{idx}", F + 72704, [128, 64, NSEL], BF16),
            )

        ctx_init = ExitStack()
        b2row = ctx_init.enter_context(nc.sbuf_tensor("b2row", [1, D], F32))

        ev_x = k.dma("S", xs_t, x_in.rearrange("(c p) f -> p c f", p=128))
        ev_c = []
        for dst, src in ((wr_b, wr_b_i), (iota320, iota320_c), (ident_s, ident_f),
                         (n1w, n1w_i), (n1b, n1b_i), (n2w, n2w_i), (n2b, n2b_i),
                         (b1c, b1_i), (br_c, br_i), (flag_c, flag_i),
                         (ones_r, ones_row_c), (b2row, b2_i)):
            ev_c.append(k.dma("C", dst[:], src))
        for dst, src in ((ident_b, ident_bf), (tri_s, tri_bf), (lt_s, lt_bf),
                         (ones_cb, ones_col_c)):
            ev_c.append(k.dma("C", dst[:], src))
        ev_eps = k.op("G", lambda g: g.memset(eps_c[:], EPS), tag=True)
        k.wait("C", ev_eps)

        k.waits("V", ev_c[:1])
        ev = k.op("V", lambda v: v.tensor_reduce(swr_c[0:1, :], wr_b[0:1, :], AX.X,
                                                 ALU.add), tag=True)
        k.waits("T", [ev, ev_c[10]])
        evt = k.op("T", lambda t: t.matmul(p512[0].ap[:, 0:1], ones_r[:],
                                           swr_c[0:1, 0:1], start=True, stop=True),
                   tag=True)
        k.wait("V", evt)
        ev_swr = k.op("V", lambda v: v.tensor_copy(swr_c[:], p512[0].ap[:, 0:1]),
                      tag=True)
        ev_b2 = []
        for fg in range(4):
            k.waits("T", [ev_c[11], ev_swr])
            evt = k.op("T", lambda t, fg=fg: t.matmul(
                p512[1 + (fg % 2)].ap[:], ones_r[:], b2row[0:1, 512*fg:512*(fg+1)],
                start=True, stop=True), tag=True)
            k.wait("V", evt)
            ev_b2.append(k.op("V", lambda v, fg=fg: v.tensor_copy(
                b2b[:, 512*fg:512*(fg+1)], p512[1 + (fg % 2)].ap[:]), tag=True))
        p512[0].free_ev = ev_swr
        p512[1].free_ev = ev_b2[2]
        p512[2].free_ev = ev_b2[3]
        ctx_init.close()
        k.barrier()

        def ln_stats(get_chunk, ev_ready, mu_t, rsig_t, nmr_t,
                     extra_router=False, router_scr=None):
            ev_r0 = []
            for c in range(NCH):
                if callable(ev_ready):
                    k.waits("V", ev_ready(c))
                else:
                    k.waits("V", ev_ready if isinstance(ev_ready, list)
                            else [ev_ready])
                xc = get_chunk(c)
                ev = None
                for g in range(4):
                    ev = k.op("V", lambda v, xc=xc, g=g: v.bn_stats(
                        st6[:, g, :], xc[:, 512*g:512*(g+1)]),
                        tag=(g == 3))
                k.wait("V", ev)
                flat = st6.ap().rearrange("p g f -> p (g f)")
                ev = None
                for i in range(8):
                    k.op("V", lambda v, i=i, flat=flat: v.tensor_copy(
                        mcol[:, i:i+1], flat[:, 3*i+1:3*i+2]))
                    ev = k.op("V", lambda v, i=i, flat=flat: v.tensor_copy(
                        vcol[:, i:i+1], flat[:, 3*i+2:3*i+3]), tag=(i == 7))
                k.wait("V", ev)
                ev = k.op("V", lambda v: v.tensor_tensor(msq[:], mcol[:], mcol[:],
                                                         ALU.mult), tag=True)
                k.wait("V", ev)
                k.op("V", lambda v, c=c: v.tensor_reduce(mu_t[:, c:c+1], mcol[:],
                                                         AX.X, ALU.add))
                k.op("V", lambda v: v.tensor_reduce(c1[:], vcol[:], AX.X, ALU.add))
                ev = k.op("V", lambda v: v.tensor_reduce(c2[:], msq[:], AX.X,
                                                         ALU.add), tag=True)
                k.wait("V", ev)
                k.op("V", lambda v, c=c: v.tensor_scalar(
                    mu_t[:, c:c+1], mu_t[:, c:c+1], 0.125, None, ALU.mult))
                k.op("V", lambda v: v.tensor_scalar(c1[:], c1[:], 1.0/2048.0, None,
                                                    ALU.mult))
                ev = k.op("V", lambda v: v.tensor_scalar(c2[:], c2[:], 0.125, None,
                                                         ALU.mult), tag=True)
                k.wait("V", ev)
                ev = k.op("V", lambda v, c=c: v.tensor_tensor(
                    c3[:], mu_t[:, c:c+1], mu_t[:, c:c+1], ALU.mult), tag=True)
                k.wait("V", ev)
                ev = k.op("V", lambda v: v.tensor_tensor(c2[:], c2[:], c3[:],
                                                         ALU.subtract), tag=True)
                k.wait("V", ev)
                ev = k.op("V", lambda v, c=c: v.tensor_tensor(
                    var8[:, c:c+1], c1[:], c2[:], ALU.add), tag=True)
                k.wait("V", ev)
                if extra_router:
                    ev_r0.append(k.op("V", lambda v, xc=xc, c=c:
                                      v.scalar_tensor_tensor(
                        router_scr[:], xc[:], 1.0, wr_b[:],
                        ALU.bypass, ALU.mult, accum_out=r08[:, c:c+1]),
                        tag=True))
            ev = k.op("V", lambda v: v.tensor_copy(t8a[:], var8[:]), tag=True)
            k.wait("C", ev)
            ev = k.op("C", lambda s: s.activation(t8b[:], var8[:], AF.Sqrt,
                                                  bias=eps_c[:], scale=1.0), tag=True)
            k.wait("V", ev)
            ev = k.op("V", lambda v: v.reciprocal(rsig_t[:], t8b[:]), tag=True)
            k.wait("V", ev)
            ev = k.op("V", lambda v: v.tensor_scalar(nmr_t[:], mu_t[:], -1.0, None,
                                                     ALU.mult), tag=True)
            k.wait("V", ev)
            ev = k.op("V", lambda v: v.tensor_tensor(nmr_t[:], nmr_t[:], rsig_t[:],
                                                     ALU.mult), tag=True)
            return ev, ev_r0

        def transpose_tile(src_ap, dst_ap, pb, waits=(), dtype_bf=True,
                           fuse=None):
            idt = ident_b if dtype_bf else ident_s
            w = list(waits)
            if pb.free_ev is not None:
                w.append(pb.free_ev)
            n = src_ap.shape[-1]
            pslice = pb.ap[:, 0:n] if dtype_bf else pb.ap[:, 0:n]
            evt = k.op("T", lambda t: t.transpose(pslice, src_ap, idt[:]),
                       tag=True, waits=w)
            k.wait("V", evt)
            if fuse is None:
                ev = k.op("V", lambda v: v.tensor_copy(dst_ap, pslice), tag=True)
            else:
                ev = k.op("V", lambda v: fuse(v, dst_ap, pslice), tag=True)
            pb.free_ev = ev
            return ev

        def emit_ffn(idx, x_view, ev_x_ready, pre=None):
            B = ffn_bufs(idx)
            pt_sb, ptl_sb, xgT, vals = B["pt"], B["ptl"], B["xgT"], B["vals"]
            wb1, wb2, cs512 = B["wb1"], B["wb2"], B["cs512"]
            wall, wbc, lnx_sb, hfT = B["wall"], B["wbc"], B["lnx"], B["hfT"]
            scrF = at(f"scrF{idx}", 64 * KB + 80896, [128, D], F32)
            cin, cout = ccw_in[idx], ccw_out[idx]
            QW = "S" if idx == 1 else "G"
            W1r = (w1bf if idx == 1 else W1).rearrange("(fc p) h -> p fc h",
                                                       p=128)

            def w1_dma(bi, hb, waits=()):
                evs = list(waits)
                ev = None
                for q in range(4):
                    ev = k.dma(QW, wb1[bi][:, 4*q:4*(q+1), :],
                               W1r[:, 4*q:4*(q+1), 256*hb:256*(hb+1)],
                               waits=(evs if q == 0 else ()))
                return ev
            pcw = [pc_ev[0]] if (idx == 1 and pc_ev[0]) else []
            ev_wb = [w1_dma(0, 0, pcw), w1_dma(1, 1, pcw)]
            if pre is None:
                ev_stats, ev_r0 = ln_stats(lambda c: x_view[:, c, :], ev_x_ready,
                                           mu8, rsig8, nmr8, extra_router=True,
                                           router_scr=scrF)
            else:
                ev_stats, ev_r0 = pre
            ev = k.op("V", lambda v: v.tensor_scalar(t8a[:], mu8[:], swr_c[:], None,
                                                     ALU.mult),
                      tag=True, waits=[ev_stats, ev_swr] + ev_r0)
            k.wait("V", ev)
            ev = k.op("V", lambda v: v.tensor_tensor(t8b[:], r08[:], t8a[:],
                                                     ALU.subtract), tag=True)
            k.wait("V", ev)
            ev = k.op("V", lambda v: v.tensor_tensor(t8b[:], t8b[:], rsig8[:],
                                                     ALU.mult), tag=True)
            k.wait("V", ev)
            ev_w = k.op("V", lambda v: v.tensor_scalar(w_own[:], t8b[:], br_c[:],
                                                       None, ALU.add), tag=True)
            if DBG_BAR:
                k.barrier()
            ev_d1 = k.dma("S", cin, w_own[:], waits=[ev_w])
            ev_cc = k.op("G", lambda g: g.collective_compute(
                "AllGather", ALU.bypass, replica_groups=RG2, ins=[cin],
                outs=[cout]), tag=True, waits=[ev_d1])
            ev_d2 = k.dma("S", wall[:],
                          cout.rearrange("a p c -> (a p c)")[None, :],
                          waits=[ev_cc])
            ctxw_dead = [ev_d2]
            ev_bc = []
            for q in range(4):
                pbq = p512[q % 2]
                wl = [ev_d2]
                if pbq.free_ev is not None:
                    wl.append(pbq.free_ev)
                evt = k.op("T", lambda t, q=q, pbq=pbq: t.matmul(
                    pbq.ap[:], ones_r[:], wall[0:1, 512*q:512*(q+1)],
                    start=True, stop=True), tag=True, waits=wl)
                k.wait("V", evt)
                ev = k.op("V", lambda v, q=q, pbq=pbq: v.tensor_copy(
                    wbc[:, 512*q:512*(q+1)], pbq.ap[:]), tag=True)
                pbq.free_ev = ev
                ev_bc.append(ev)
            k.waits("V", ev_bc)
            ev_ge = []
            for c in range(NCH):
                ev_ge.append(k.op("V", lambda v, c=c: v.tensor_scalar(
                    scrF[:], wbc[:], w_own[:, c:c+1], 0.0, ALU.is_ge, ALU.add,
                    accum_out=ge8[:, c:c+1]), tag=True))
            k.waits("V", ev_ge)
            ev = k.op("V", lambda v: v.tensor_scalar(sel8[:], ge8[:], 511.5, None,
                                                     ALU.is_le), tag=True)
            k.wait("V", ev)
            ev_self = k.op("V", lambda v: v.tensor_copy(sel_bf[:], sel8[:]),
                           tag=True)
            ev = k.op("V", lambda v: v.tensor_scalar(hnsr8[:], sel8[:], -0.5, 0.5,
                                                     ALU.mult, ALU.add), tag=True)
            k.wait("V", ev)
            ev = k.op("V", lambda v: v.tensor_tensor(hnsr8[:], hnsr8[:], rsig8[:],
                                                     ALU.mult), tag=True)
            k.wait("V", ev)
            ev = k.op("V", lambda v: v.tensor_scalar(nmhr8[:], mu8[:], -1.0, None,
                                                     ALU.mult), tag=True)
            k.wait("V", ev)
            ev = k.op("V", lambda v: v.tensor_tensor(nmhr8[:], nmhr8[:], hnsr8[:],
                                                     ALU.mult), tag=True)
            k.wait("V", ev)
            ev = k.op("V", lambda v: v.tensor_scalar(t8a[:], w_own[:], 0.5, None,
                                                     ALU.mult), tag=True)
            k.wait("V", ev)
            ev_ws = k.op("V", lambda v: v.tensor_tensor(ws8[:], t8a[:], sel8[:],
                                                        ALU.mult), tag=True)
            ctxw_dead.append(ev_ws)
            pbA, pbB = p512[2], p512[3]
            wl = [ev_self]
            for b_ in (pbA, pbB):
                if b_.free_ev is not None:
                    wl.append(b_.free_ev)
            evt1 = k.op("T", lambda t: t.matmul(pbA.ap[:, 0:8], lt_s[:], sel_bf[:],
                                                start=True, stop=True),
                        tag=True, waits=wl)
            evt2 = k.op("T", lambda t: t.matmul(pbB.ap[0:1, 0:8], ones_cb[:],
                                                sel_bf[:], start=True, stop=True),
                        tag=True)
            k.waits("V", [evt1, evt2])
            ev = k.op("V", lambda v: v.tensor_copy(pfx[:], pbA.ap[:, 0:8]),
                      tag=True)
            pbA.free_ev = ev
            ev = k.op("V", lambda v: v.tensor_copy(crow[:], pbB.ap[0:1, 0:8]),
                      tag=True)
            pbB.free_ev = ev
            k.wait("V", ev)
            ev = k.op("V", lambda v: v.tensor_tensor_scan(
                coff[0:1, :], crow[0:1, :], crow[0:1, :], 0.0, ALU.add,
                ALU.bypass), tag=True)
            k.wait("V", ev)
            k.op("V", lambda v: v.tensor_copy(crow[0:1, 1:8], coff[0:1, 0:7]))
            ev = k.op("V", lambda v: v.memset(crow[0:1, 0:1], 0.0), tag=True)
            wl = [ev]
            if pbA.free_ev is not None:
                wl.append(pbA.free_ev)
            evt = k.op("T", lambda t: t.matmul(pbA.ap[:, 0:8], ones_r[:],
                                               crow[0:1, :], start=True, stop=True),
                       tag=True, waits=wl)
            k.wait("V", evt)
            ev = k.op("V", lambda v: v.tensor_tensor(pfx[:], pfx[:],
                                                     pbA.ap[:, 0:8], ALU.add),
                      tag=True)
            pbA.free_ev = ev
            k.wait("V", ev)
            ev = k.op("V", lambda v: v.tensor_tensor(pfx[:], pfx[:], sel8[:],
                                                     ALU.mult), tag=True)
            k.wait("V", ev)
            ev = k.op("V", lambda v: v.tensor_tensor(pfx[:], pfx[:], sel8[:],
                                                     ALU.add), tag=True)
            k.wait("V", ev)
            ev_pfx = k.op("V", lambda v: v.tensor_scalar(pfx[:], pfx[:], 1.0, None,
                                                         ALU.subtract), tag=True)
            if DBG_BAR:
                k.barrier()
            k.wait("V", ev_pfx)
            ev_pt = []
            for c in range(NCH):
                ev = k.op("V", lambda v, c=c: v.tensor_scalar(
                    pt_sb[:, c, :], iota320[:], pfx[:, c:c+1], None, ALU.is_equal),
                    tag=True, waits=[ev_pfx])
                ev_pt.append(ev)
            ev_ptl = {}
            for c in range(NCH):
                ev = k.op("V", lambda v, c=c: v.tensor_scalar(
                    scrpt[:], pt_sb[:, c, :], ws8[:, c:c+1], None,
                    ALU.mult), tag=True, waits=[ev_pt[c], ev_ws])
                k.wait("V", ev)
                for jc in range(NJC):
                    pb = pbf[jc % 2]
                    jn = JSZ[jc]
                    wl = [ev]
                    if pb.free_ev is not None:
                        wl.append(pb.free_ev)
                    evt = k.op("T", lambda t, c=c, jc=jc, pb=pb, jn=jn: t.transpose(
                        pb.ap[0:jn, 0:128], scrpt[:, 128*jc:128*jc+jn],
                        ident_b[:]), tag=True, waits=wl)
                    k.wait("V", evt)
                    ev2 = k.op("V", lambda v, c=c, jc=jc, pb=pb, jn=jn:
                               v.tensor_copy(ptl_sb[0:jn, jc*NCH + c, :],
                                             pb.ap[0:jn, 0:128]), tag=True)
                    pb.free_ev = ev2
                    ev_ptl[(jc, c)] = ev2
            ev_lnx = []
            for c in range(NCH):
                ev = k.op("C", lambda s, c=c: s.activation(
                    lnx_sb[:, c, :], x_view[:, c, :], AF.Identity,
                    bias=nmr8[:, c:c+1], scale=rsig8[:, c:c+1]),
                    tag=True, waits=[ev_stats] + ctxw_dead)
                ev_lnx.append(ev)
            for fc in range(NFC):
                pb = p512[fc % 3]
                wl = list(ev_pt) + [ev_lnx[-1]]
                if pb.free_ev is not None:
                    wl.append(pb.free_ev)
                evt = None
                for c in range(NCH):
                    e = k.op("T", lambda t, fc=fc, c=c, pb=pb: t.matmul(
                        pb.ap[:, 0:NSEL], lnx_sb[:, c, 128*fc:128*(fc+1)],
                        pt_sb[:, c, :], start=(c == 0), stop=(c == NCH - 1)),
                        tag=(c == NCH - 1), waits=(wl if c == 0 else ()))
                    if e:
                        evt = e
                k.wait("V", evt)
                ev = k.op("V", lambda v, fc=fc, pb=pb: v.tensor_copy(
                    xgT[:, fc, :], pb.ap[:, 0:NSEL]), tag=True)
                pb.free_ev = ev
            ev_xg = ev
            k.waits("V", [evt])
            if DBG_BAR:
                k.barrier()
            if DBG_BAR:
                k.barrier()
            last_use = [None, None]
            ev_hf = None
            for hb in range(32):
                bi = hb % 2
                buf = wb1[bi]
                evt = None
                for h2 in range(2):
                    hc = 2*hb + h2
                    pb = p512[hc % 3]
                    wl = [ev_wb[bi], ev_xg]
                    if pb.free_ev is not None:
                        wl.append(pb.free_ev)
                    for fc in range(NFC):
                        e = k.op("T", lambda t, fc=fc, pb=pb, h2=h2, buf=buf:
                                 t.matmul(pb.ap[:, 0:NSEL],
                                          buf[:, fc, 128*h2:128*(h2+1)],
                                          xgT[:, fc, :], start=(fc == 0),
                                          stop=(fc == NFC - 1)),
                                 tag=(fc == NFC - 1),
                                 waits=(wl if fc == 0 else ()))
                        if e:
                            evt = e
                    k.wait("C", evt)
                    ev_hf = k.op("C", lambda s, hc=hc, pb=pb: s.activation(
                        hfT[:, hc, :], pb.ap[:, 0:NSEL], AF.Gelu_apprx_tanh,
                        bias=b1c[:, hc:hc+1], scale=1.0), tag=True)
                    pb.free_ev = ev_hf
                last_use[bi] = evt
                if hb + 2 < 32:
                    ev_wb[bi] = w1_dma(bi, hb + 2, waits=[last_use[bi]])
            W2r = (w2bf if idx == 1 else W2).rearrange("(hq p) f -> p hq f",
                                                       p=128)
            ev_w2 = {}
            use_w2 = {}
            def w2_dma(i):
                fg, hq = i // 16, i % 16
                wl = [use_w2[i - 3]] if (i - 3) in use_w2 else list(pcw)
                ev_w2[i] = k.dma(QW, wb2[i % 3][:],
                                 W2r[:, 4*hq:4*(hq+1), 512*fg:512*(fg+1)],
                                 waits=wl)
            for i in range(3):
                w2_dma(i)
            if DBG_BAR:
                k.barrier()
            ev_vals = []
            for fg in range(4):
                pbs = [p512[3], p512[4], p512[5]]
                wl0 = [ev_hf]
                for b_ in pbs:
                    if b_.free_ev is not None:
                        wl0.append(b_.free_ev)
                evt = None
                for hc in range(64):
                    i = fg * 16 + hc // 4
                    h4 = hc % 4
                    buf = wb2[i % 3]
                    wl = [ev_w2[i]] + (wl0 if hc == 0 else [])
                    for jc in range(NJC):
                        jn = JSZ[jc]
                        e = k.op("T", lambda t, hc=hc, jc=jc, buf=buf, h4=h4,
                                 jn=jn, pb=pbs[jc]: t.matmul(
                            pb.ap[0:jn, :], hfT[:, hc, 128*jc:128*jc+jn],
                            buf[:, h4, :], start=(hc == 0), stop=(hc == 63)),
                            tag=((hc == 63 and jc == NJC - 1)
                                 or (h4 == 3 and jc == NJC - 1)),
                            waits=(wl if jc == 0 else ()))
                        if e:
                            evt = e
                    if h4 == 3:
                        use_w2[i] = evt
                        if i + 3 < 64:
                            w2_dma(i + 3)
                k.wait("V", evt)
                for jc in range(NJC):
                    jn = JSZ[jc]
                    ev = k.op("V", lambda v, fg=fg, jc=jc, jn=jn, pb=pbs[jc]:
                              v.tensor_tensor(vals[0:jn, jc, 512*fg:512*(fg+1)],
                                              pb.ap[0:jn, :],
                                              b2b[0:jn, 512*fg:512*(fg+1)],
                                              ALU.add), tag=True)
                    pbs[jc].free_ev = ev
                    ev_vals.append(ev)
            if DBG_BAR:
                k.barrier()
            ev_done = []
            prev3 = {fg: None for fg in range(4)}
            prev_any = [None]
            for c in range(NCH):
                for fg in range(4):
                    pb = p512[(c*4 + fg) % 3]
                    wl = [ev_vals[-1], ev_ptl[(NJC-1, c)]]
                    if pb.free_ev is not None:
                        wl.append(pb.free_ev)
                    evt = None
                    for jc in range(NJC):
                        jn = JSZ[jc]
                        e = k.op("T", lambda t, c=c, fg=fg, jc=jc, jn=jn, pb=pb:
                                 t.matmul(pb.ap[:],
                                          ptl_sb[0:jn, jc*NCH + c, :],
                                          vals[0:jn, jc, 512*fg:512*(fg+1)],
                                          start=(jc == 0), stop=(jc == NJC - 1)),
                                 tag=(jc == NJC - 1),
                                 waits=(wl if jc == 0 else ()))
                        if e:
                            evt = e
                    wl1 = [evt]
                    if prev_any[0] is not None:
                        wl1.append(prev_any[0])
                    ev1 = k.op("C", lambda s, c=c, fg=fg: s.activation(
                        cs512[:],
                        x_view[:, c, 512*fg:512*(fg+1)], AF.Identity,
                        bias=nmhr8[:, c:c+1], scale=hnsr8[:, c:c+1]), tag=True,
                        waits=wl1)
                    k.waits("V", [evt, ev1])
                    ev2 = k.op("V", lambda v, pb=pb: v.tensor_tensor(
                        cs512[:], cs512[:], pb.ap[:], ALU.add), tag=True)
                    pb.free_ev = ev2
                    k.wait("V", ev2)
                    ev3 = k.op("V", lambda v, c=c, fg=fg: v.tensor_tensor(
                        x_view[:, c, 512*fg:512*(fg+1)],
                        x_view[:, c, 512*fg:512*(fg+1)],
                        cs512[:], ALU.add), tag=True)
                    prev_any[0] = ev3
                    ev_done.append(ev3)
            return [ev_done[4*c + 3] for c in range(NCH)]


        def emit_attn(ev_x_ready, pre_s2=None):
            hT = at("hT", 64 * KB, [128, NFC, T], BF16)
            znbf2 = [at(f"znbf{i}", (96 + 8 * i) * KB, [128, D], BF16)
                     for i in range(2)]
            znb5 = at("znb5", 96 * KB, [128, 512], BF16)
            zn22 = [at(f"zn2{i}", (100 + 8 * i) * KB, [128, D], BF16)
                    for i in range(2)]
            v_sb = at("v_sb", 0, [128, NCH, D], BF16)
            ret = at("ret", 32 * KB, [128, NCH, D], BF16)
            qT = at("qT", 104 * KB, [128, NH, T], BF16)
            kT = at("kT", 136 * KB, [128, T], BF16)
            ktm2 = [at(f"ktm{i}", 138 * KB + 256 * i, [128, 128], BF16)
                    for i in range(2)]
            scT = at("scT", 139 * KB, [128, NCH, T], BF16)
            sown = at("sown", 155 * KB, [128, D], BF16)
            sp = at("sp", 159 * KB, [128, D], BF16)
            wpb = [at(f"wpb{i}", (136 + 16 * i) * KB, [128, NFC, 512], BF16)
                   for i in range(2)]
            mqb = [at(f"mqb{i}", (168 + 4 * i) * KB, [128, T], F32)
                   for i in range(2)]
            wkb = at("wkb", 163 * KB, [128, NFC, 128], BF16)
            mkb = [at(f"mkb{i}", (167 + 4 * i) * KB, [128, T], F32)
                   for i in range(2)]
            gsb = at("gsb", 0, [128, NCH, D], BF16)
            wgb = [at("wgb0", 136 * KB, [128, NFC, 512], BF16),
                   at("wgb1", 163 * KB, [128, NFC, 512], BF16)]
            uT = at("uT", 32 * KB, [128, NFC, T], BF16)
            wob = [at(f"wob{i}", (64 + 16 * i) * KB, [128, NFC, 512], BF16)
                   for i in range(2)]
            xblk = at("xblk", 100 * KB, [128, NCH, D], F32)
            zn4 = at("zn4", 64 * KB, [128, D], BF16)
            h2T = at("h2T", 68 * KB, [128, NFC, T], BF16)
            u2 = at("u2", 0, [128, NHA, 512], BF16)
            awb = [at(f"awb{i}", (44 + 4 * i) * KB, [128, NFC, 128], BF16)
                   for i in range(2)]
            aw2b = [at(f"aw2b{i}", (52 + 4 * i) * KB, [128, NFC, 128], BF16)
                    for i in range(2)]
            aTs = at("aTs", 60 * KB, [128, 512], BF16)
            aw3b = [at(f"aw3b{i}", (61 + 1 * i) * KB, [128, 512], BF16)
                    for i in range(2)]
            xbuf = at("xbuf", 164 * KB, [128, D], F32)
            scrA = at("scrA", 168 * KB, [128, D], F32)

            if pre_s2 is None:
                ev_s2, _ = ln_stats(lambda c: xs_t[:, c, :], ev_x_ready,
                                    mu2, rsig2, nmr2, extra_router=False)
            else:
                ev_s2 = pre_s2
            ev_ht = None
            ev_xpage = []
            prevA = {0: [], 1: []}
            prevC = [None]
            for c in range(NCH):
                znbf = znbf2[c % 2 if PHA_DB else 0]
                zn2 = zn22[c % 2 if PHA_DB else 0]
                if prevC[0] is not None:
                    k.wait("V", prevC[0])
                ev1 = k.op("C", lambda s, c=c, znbf=znbf: s.activation(
                    znbf[:], xs_t[:, c, :], AF.Identity,
                    bias=nmr2[:, c:c+1], scale=rsig2[:, c:c+1]),
                    tag=True, waits=[ev_s2] + prevA[c % 2])
                ev_znd = k.dma("S", zn_page[:, c, :], znbf[:], waits=[ev1])
                k.wait("V", ev1)
                ev = None
                for g in range(4):
                    ev = k.op("V", lambda v, g=g: v.bn_stats(
                        st6[:, g, :], znbf[:, 512*g:512*(g+1)]), tag=(g == 3))
                k.wait("V", ev)
                flat = st6.ap().rearrange("p g f -> p (g f)")
                ev = None
                for i in range(8):
                    k.op("V", lambda v, i=i, flat=flat: v.tensor_copy(
                        mcol[:, i:i+1], flat[:, 3*i+1:3*i+2]))
                    ev = k.op("V", lambda v, i=i, flat=flat: v.tensor_copy(
                        vcol[:, i:i+1], flat[:, 3*i+2:3*i+3]), tag=(i == 7))
                k.wait("V", ev)
                ev = k.op("V", lambda v: v.tensor_tensor(msq[:], mcol[:], mcol[:],
                                                         ALU.mult), tag=True)
                k.wait("V", ev)
                k.op("V", lambda v: v.tensor_reduce(c1[:], mcol[:], AX.X, ALU.add))
                k.op("V", lambda v: v.tensor_reduce(c2[:], vcol[:], AX.X, ALU.add))
                ev = k.op("V", lambda v: v.tensor_reduce(c3[:], msq[:], AX.X,
                                                         ALU.add), tag=True)
                k.wait("V", ev)
                k.op("V", lambda v: v.tensor_scalar(c1[:], c1[:], 0.125, None,
                                                    ALU.mult))
                k.op("V", lambda v: v.tensor_scalar(c2[:], c2[:], 1.0/2048.0, None,
                                                    ALU.mult))
                ev = k.op("V", lambda v: v.tensor_scalar(c3[:], c3[:], 0.125, None,
                                                         ALU.mult), tag=True)
                k.wait("V", ev)
                ev = k.op("V", lambda v: v.tensor_tensor(msq[:, 0:1], c1[:], c1[:],
                                                         ALU.mult), tag=True)
                k.wait("V", ev)
                ev = k.op("V", lambda v: v.tensor_tensor(c2[:], c2[:], c3[:],
                                                         ALU.add), tag=True)
                k.wait("V", ev)
                ev = k.op("V", lambda v: v.tensor_tensor(c2[:], c2[:], msq[:, 0:1],
                                                         ALU.subtract), tag=True)
                k.wait("C", ev)
                ev = k.op("C", lambda s: s.activation(c2[:], c2[:], AF.Sqrt,
                                                      bias=eps_c[:], scale=1.0),
                          tag=True)
                k.wait("V", ev)
                ev = k.op("V", lambda v: v.reciprocal(c2[:], c2[:]), tag=True)
                k.wait("V", ev)
                ev = k.op("V", lambda v: v.tensor_tensor(c3[:], c1[:], c2[:],
                                                         ALU.mult), tag=True)
                k.wait("V", ev)
                ev = k.op("V", lambda v: v.tensor_scalar(c3[:], c3[:], -1.0, None,
                                                         ALU.mult), tag=True)
                k.wait("C", ev)
                ev2 = k.op("C", lambda s, zn2=zn2, znbf=znbf: s.activation(
                    zn2[:], znbf[:], AF.Identity, bias=c3[:], scale=c2[:]),
                    tag=True)
                prevC[0] = ev2
                lastT = None
                for fc in range(NFC):
                    pb = pbf[fc % 2]
                    wl = [ev2]
                    if pb.free_ev is not None:
                        wl.append(pb.free_ev)
                    evt = k.op("T", lambda t, fc=fc, pb=pb, zn2=zn2: t.transpose(
                        pb.ap[:, 0:128], zn2[:, 128*fc:128*(fc+1)], ident_b[:]),
                        tag=True, waits=wl)
                    lastT = evt
                    k.wait("V", evt)
                    ev3 = k.op("V", lambda v, c=c, fc=fc, pb=pb: v.tensor_scalar(
                        hT[:, fc, 128*c:128*(c+1)], pb.ap[:, 0:128],
                        n1w[:, fc:fc+1], n1b[:, fc:fc+1], ALU.mult, ALU.add),
                        tag=True)
                    pb.free_ev = ev3
                    ev_ht = ev3
                ev_xpage.append(k.dma("S", x_page[:, c, :], xs_t[:, c, :],
                                      waits=[ev2]))
                prevA[c % 2] = [ev_znd, lastT]
            k.barrier()

            wvr = {0: wv, 1: wg, 2: wo}

            def wblk_dma(which, og, dst, wl):
                src = wvr[which]
                srcr = src.rearrange("(fc p) f -> p fc f", p=128)
                ev = None
                for q in range(4):
                    ev = k.dma("G", dst[:, 4*q:4*(q+1), :],
                               srcr[:, 4*q:4*(q+1), 512*og:512*(og+1)],
                               waits=(wl if q == 0 else ()))
                return ev

            ev_wv = [wblk_dma(0, 0, wpb[0], []), wblk_dma(0, 1, wpb[1], [])]
            last_v = [None, None]
            ev_vdone = None
            for og in range(4):
                bi = og % 2
                evt = None
                for c in range(NCH):
                    pb = p512[c % 3]
                    wl = [ev_wv[bi], ev_ht]
                    if pb.free_ev is not None:
                        wl.append(pb.free_ev)
                    for fc in range(NFC):
                        e = k.op("T", lambda t, c=c, fc=fc, og=og, pb=pb,
                                 buf=wpb[bi]: t.matmul(
                            pb.ap[:], hT[:, fc, 128*c:128*(c+1)],
                            buf[:, fc, :], start=(fc == 0), stop=(fc == NFC - 1)),
                            tag=(fc == NFC - 1), waits=(wl if fc == 0 else ()))
                        if e:
                            evt = e
                    k.wait("V", evt)
                    ev_vdone = k.op("V", lambda v, c=c, og=og, pb=pb:
                                    v.tensor_copy(v_sb[:, c, 512*og:512*(og+1)],
                                                  pb.ap[:]), tag=True)
                    pb.free_ev = ev_vdone
                last_v[bi] = evt
                if og + 2 < 4:
                    ev_wv[bi] = wblk_dma(0, og + 2, wpb[bi], [last_v[bi]])

            wqr = wq.rearrange("(fc p) f -> p fc f", p=128)
            ev_mq = [k.dma("S", mqb[0][:], mq_i[0]), k.dma("S", mqb[1][:], mq_i[1])]

            def wq_dma(bi, hb, wl):
                ev = None
                for q in range(4):
                    ev = k.dma("G", wpb[bi][:, 4*q:4*(q+1), :],
                               wqr[:, 4*q:4*(q+1), 512*hb:512*(hb+1)],
                               waits=(wl if q == 0 else ()))
                return ev
            ev_wq = [wq_dma(0, 0, [last_v[0]]), wq_dma(1, 1, [last_v[1]])]
            last_q = [None, None]
            for hb in range(4):
                bi = hb % 2
                evt = None
                for h4 in range(4):
                    h = 4 * hb + h4
                    for sg in range(2):
                        pb = p512[(h4 * 2 + sg) % 3]
                        wl = [ev_wq[bi], ev_mq[h % 2]]
                        if pb.free_ev is not None:
                            wl.append(pb.free_ev)
                        for fc in range(NFC):
                            e = k.op("T", lambda t, fc=fc, h4=h4, sg=sg, pb=pb,
                                     buf=wpb[bi]: t.matmul(
                                pb.ap[:], buf[:, fc, 128*h4:128*(h4+1)],
                                hT[:, fc, 512*sg:512*(sg+1)],
                                start=(fc == 0), stop=(fc == NFC - 1)),
                                tag=(fc == NFC - 1),
                                waits=(wl if fc == 0 else ()))
                            if e:
                                evt = e
                        k.wait("V", evt)
                        ev = k.op("V", lambda v, h=h, sg=sg, pb=pb: v.tensor_tensor(
                            qT[:, h, 512*sg:512*(sg+1)], pb.ap[:],
                            mqb[h % 2][:, 512*sg:512*(sg+1)], ALU.mult),
                            tag=True)
                        pb.free_ev = ev
                    if h + 2 < NH:
                        ev_mq[h % 2] = k.dma("S", mqb[h % 2][:], mq_i[h + 2],
                                             waits=[ev])
                last_q[bi] = evt
                if hb + 2 < 4:
                    ev_wq[bi] = wq_dma(bi, hb + 2, [last_q[bi]])
            k.barrier()

            wkr = wk.rearrange("(fc p) f -> p fc f", p=128)
            ev_mk = [k.dma("S", mkb[0][:], mk_i[0]), k.dma("S", mkb[1][:], mk_i[1])]
            ev_state = None
            prev_head_done = None
            for h in range(NH):
                wlk = [prev_head_done] if prev_head_done else []
                ev_wk = None
                for q in range(4):
                    ev_wk = k.dma("G", wkb[:, 4*q:4*(q+1), :],
                                  wkr[:, 4*q:4*(q+1), 128*h:128*(h+1)],
                                  waits=(wlk if q == 0 else ()))
                k.dma("G", w1bf[128*h:128*(h+1), :], W1[128*h:128*(h+1), :])
                pc_ev[0] = k.dma("G", w2bf[512*h:512*(h+1), :],
                                 W2[512*h:512*(h+1), :])
                ev_kT = None
                for sg in range(2):
                    pb = p512[sg % 2]
                    wl = [ev_wk, ev_mk[h % 2]]
                    if pb.free_ev is not None:
                        wl.append(pb.free_ev)
                    evt = None
                    for fc in range(NFC):
                        e = k.op("T", lambda t, fc=fc, sg=sg, pb=pb: t.matmul(
                            pb.ap[:], wkb[:, fc, :],
                            hT[:, fc, 512*sg:512*(sg+1)],
                            start=(fc == 0), stop=(fc == NFC - 1)),
                            tag=(fc == NFC - 1), waits=(wl if fc == 0 else ()))
                        if e:
                            evt = e
                    k.wait("V", evt)
                    ev_kT = k.op("V", lambda v, sg=sg, pb=pb, h=h: v.tensor_tensor(
                        kT[:, 512*sg:512*(sg+1)], pb.ap[:],
                        mkb[h % 2][:, 512*sg:512*(sg+1)], ALU.mult), tag=True)
                    pb.free_ev = ev_kT
                if h + 2 < NH:
                    ev_mk[h % 2] = k.dma("S", mkb[h % 2][:], mk_i[h + 2],
                                         waits=[ev_kT])
                pS = p512[2]
                wlS = [ev_kT]
                if pS.free_ev is not None:
                    wlS.append(pS.free_ev)
                ev_ktm_prev = {0: None, 1: None}
                ev_state_last = None
                for c in range(NCH):
                    pb = pbf[c % 2]
                    ktm = ktm2[c % 2 if KTM_DB else 0]
                    wl = [ev_kT]
                    if pb.free_ev is not None:
                        wl.append(pb.free_ev)
                    evt = k.op("T", lambda t, c=c, pb=pb: t.transpose(
                        pb.ap[:, 0:128], kT[:, 128*c:128*(c+1)], ident_b[:]),
                        tag=True, waits=wl)
                    k.wait("V", evt)
                    evk = k.op("V", lambda v, pb=pb, ktm=ktm: v.tensor_copy(
                        ktm[:], pb.ap[:, 0:128]), tag=True,
                        waits=[ev_ktm_prev[c % 2]] if ev_ktm_prev[c % 2] else [])
                    pb.free_ev = evk
                    wl2 = [evk] + (wlS if c == 0 else [])
                    ev_state_last = k.op("T", lambda t, c=c, h=h, pS=pS, ktm=ktm:
                                         t.matmul(
                        pS.ap[:, 0:128], ktm[:], v_sb[:, c, 128*h:128*(h+1)],
                        start=(c == 0), stop=(c == NCH - 1)),
                        tag=True, waits=wl2)
                    ev_ktm_prev[c % 2] = ev_state_last
                ev_ktm_prev = ev_state_last
                k.wait("V", ev_ktm_prev)
                ev_state = k.op("V", lambda v, h=h, pS=pS: v.tensor_copy(
                    sown[:, 128*h:128*(h+1)], pS.ap[:, 0:128]), tag=True)
                pS.free_ev = ev_state
                ev_sc = []
                for tc in range(NCH):
                    s0 = 128 * tc
                    first = True
                    while s0 < T:
                        wdt = min(512, T - s0)
                        pb = p512[3 + (len(ev_sc) % 2)]
                        wl = [ev_kT]
                        if pb.free_ev is not None:
                            wl.append(pb.free_ev)
                        evt = k.op("T", lambda t, tc=tc, s0=s0, wdt=wdt, h=h,
                                   pb=pb: t.matmul(
                            pb.ap[:, 0:wdt], kT[:, 128*tc:128*(tc+1)],
                            qT[:, h, s0:s0+wdt], start=True, stop=True),
                            tag=True, waits=wl)
                        k.wait("V", evt)
                        if first:
                            ev = k.op("V", lambda v, tc=tc, s0=s0, pb=pb:
                                      v.tensor_tensor(
                                scT[:, tc, s0:s0+128], pb.ap[:, 0:128],
                                tri_s[:], ALU.mult), tag=True)
                            if wdt > 128:
                                ev = k.op("V", lambda v, tc=tc, s0=s0, wdt=wdt,
                                          pb=pb: v.tensor_copy(
                                    scT[:, tc, s0+128:s0+wdt],
                                    pb.ap[:, 128:wdt]), tag=True)
                        else:
                            ev = k.op("V", lambda v, tc=tc, s0=s0, wdt=wdt, pb=pb:
                                      v.tensor_copy(scT[:, tc, s0:s0+wdt],
                                                    pb.ap[:, 0:wdt]), tag=True)
                        pb.free_ev = ev
                        ev_sc.append(ev)
                        s0 += wdt
                        first = False
                for sc in range(NCH):
                    pb = p512[5]
                    wl = [ev_sc[-1]]
                    if pb.free_ev is not None:
                        wl.append(pb.free_ev)
                    evt = None
                    for tc in range(sc + 1):
                        e = k.op("T", lambda t, tc=tc, sc=sc, h=h, pb=pb:
                                 t.matmul(
                            pb.ap[:, 0:128], scT[:, tc, 128*sc:128*(sc+1)],
                            v_sb[:, tc, 128*h:128*(h+1)],
                            start=(tc == 0), stop=(tc == sc)),
                            tag=(tc == sc), waits=(wl if tc == 0 else ()))
                        if e:
                            evt = e
                    k.wait("V", evt)
                    ev = k.op("V", lambda v, sc=sc, h=h, pb=pb: v.tensor_copy(
                        ret[:, sc, 128*h:128*(h+1)], pb.ap[:, 0:128]), tag=True)
                    pb.free_ev = ev
                prev_head_done = ev
            ev_d1 = k.dma("S", ccs_in, sown[:], waits=[ev_state])

            ev_wg = [wblk_dma(1, 0, wgb[0], [prev_head_done, ev_d1]),
                     wblk_dma(1, 1, wgb[1], [])]
            last_g = [None, None]
            ev_g = None
            for og in range(4):
                bi = og % 2
                evt = None
                for c in range(NCH):
                    pb = p512[c % 3]
                    wl = [ev_wg[bi]]
                    if pb.free_ev is not None:
                        wl.append(pb.free_ev)
                    for fc in range(NFC):
                        e = k.op("T", lambda t, c=c, fc=fc, pb=pb, buf=wgb[bi]:
                                 t.matmul(
                            pb.ap[:], hT[:, fc, 128*c:128*(c+1)],
                            buf[:, fc, :], start=(fc == 0), stop=(fc == NFC - 1)),
                            tag=(fc == NFC - 1), waits=(wl if fc == 0 else ()))
                        if e:
                            evt = e
                    k.wait("C", evt)
                    ev_g = k.op("C", lambda s, c=c, og=og, pb=pb: s.activation(
                        gsb[:, c, 512*og:512*(og+1)], pb.ap[:], AF.Silu),
                        tag=True)
                    pb.free_ev = ev_g
                last_g[bi] = evt
                if og + 2 < 4:
                    ev_wg[bi] = wblk_dma(1, og + 2, wgb[bi], [last_g[bi]])
            ev_cc = k.op("G", lambda g: g.collective_compute(
                "AllGather", ALU.bypass, replica_groups=RG2, ins=[ccs_in],
                outs=[ccs_out]), tag=True, waits=[ev_d1])
            ev_d2 = k.dma("S", sp[:], ccs_out[0], waits=[ev_cc])
            k.waits("V", [ev_d2])
            ev_sp = k.op("V", lambda v: v.tensor_scalar(
                sp[:], sp[:], flag_c[:, 0:1], None, ALU.mult), tag=True)
            ev_apply = None
            for h in range(NH):
                for sc in range(NCH):
                    pb = p512[(h * NCH + sc) % 3]
                    wl = [ev_sp, prev_head_done]
                    if pb.free_ev is not None:
                        wl.append(pb.free_ev)
                    evt = k.op("T", lambda t, h=h, sc=sc, pb=pb: t.matmul(
                        pb.ap[:, 0:128], qT[:, h, 128*sc:128*(sc+1)],
                        sp[:, 128*h:128*(h+1)], start=True, stop=True),
                        tag=True, waits=wl)
                    k.wait("V", evt)
                    ev_apply = k.op("V", lambda v, h=h, sc=sc, pb=pb:
                                    v.tensor_tensor(
                        ret[:, sc, 128*h:128*(h+1)], ret[:, sc, 128*h:128*(h+1)],
                        pb.ap[:, 0:128], ALU.add), tag=True)
                    pb.free_ev = ev_apply
            k.barrier()
            ev_u = None
            for c in range(NCH):
                ev1 = k.op("C", lambda s, c=c: s.activation(
                    scrA[:], ret[:, c, :], AF.Square), tag=True,
                    waits=[ev_g, ev_apply] + ([ev_u] if ev_u else []))
                k.waits("V", [ev1])
                ev = None
                for h in range(NH):
                    k.op("V", lambda v, c=c, h=h: v.tensor_reduce(
                        gnS[:, h:h+1], ret[:, c, 128*h:128*(h+1)], AX.X, ALU.add))
                    ev = k.op("V", lambda v, h=h: v.tensor_reduce(
                        gnQ[:, h:h+1], scrA[:, 128*h:128*(h+1)], AX.X, ALU.add),
                        tag=(h == NH - 1))
                k.wait("V", ev)
                ev = k.op("V", lambda v: v.tensor_scalar(
                    gnS[:], gnS[:], 1.0/128.0, None, ALU.mult), tag=True)
                k.wait("V", ev)
                ev = k.op("V", lambda v: v.tensor_tensor(gnA[:], gnS[:], gnS[:],
                                                         ALU.mult), tag=True)
                k.wait("V", ev)
                ev = k.op("V", lambda v: v.tensor_scalar(
                    gnQ[:], gnQ[:], 1.0/128.0, None, ALU.mult), tag=True)
                k.wait("V", ev)
                ev = k.op("V", lambda v: v.tensor_tensor(gnQ[:], gnQ[:], gnA[:],
                                                         ALU.subtract), tag=True)
                k.wait("C", ev)
                ev = k.op("C", lambda s: s.activation(gnQ[:], gnQ[:], AF.Sqrt,
                                                      bias=eps_c[:], scale=1.0),
                          tag=True)
                k.wait("V", ev)
                ev = k.op("V", lambda v: v.reciprocal(gnA[:], gnQ[:]), tag=True)
                k.wait("V", ev)
                ev = k.op("V", lambda v: v.tensor_tensor(gnB[:], gnS[:], gnA[:],
                                                         ALU.mult), tag=True)
                k.wait("V", ev)
                evb = k.op("V", lambda v: v.tensor_scalar(gnB[:], gnB[:], -1.0,
                                                          None, ALU.mult),
                           tag=True)
                k.wait("V", evb)
                for h in range(NH):
                    ev = k.op("V", lambda v, c=c, h=h: v.tensor_scalar(
                        scrA[:, 128*h:128*(h+1)], ret[:, c, 128*h:128*(h+1)],
                        gnA[:, h:h+1], gnB[:, h:h+1], ALU.mult, ALU.add),
                        tag=(h == NH - 1))
                k.wait("V", ev)
                ev_u = k.op("V", lambda v, c=c: v.tensor_tensor(
                    gsb[:, c, :], gsb[:, c, :], scrA[:], ALU.mult), tag=True)
            k.barrier()

            ev_uT = None
            for c in range(NCH):
                for fc in range(NFC):
                    pb = pbf[fc % 2]
                    wl = [ev_u]
                    if pb.free_ev is not None:
                        wl.append(pb.free_ev)
                    evt = k.op("T", lambda t, c=c, fc=fc, pb=pb: t.transpose(
                        pb.ap[:, 0:128], gsb[:, c, 128*fc:128*(fc+1)],
                        ident_b[:]), tag=True, waits=wl)
                    k.wait("V", evt)
                    ev_uT = k.op("V", lambda v, c=c, fc=fc, pb=pb: v.tensor_copy(
                        uT[:, fc, 128*c:128*(c+1)], pb.ap[:, 0:128]), tag=True)
                    pb.free_ev = ev_uT
            ev_wo = [wblk_dma(2, 0, wob[0], [ev_uT]), wblk_dma(2, 1, wob[1], [])]
            last_o = [None, None]
            ev_xblk = None
            for og in range(4):
                bi = og % 2
                evt = None
                for c in range(NCH):
                    ev_zn = k.dma("S", znb5[:], zn_page[:, c, 512*og:512*(og+1)],
                                  waits=([ev_xblk] if ev_xblk else []))
                    pb = p512[c % 3]
                    wl = [ev_wo[bi], ev_uT]
                    if pb.free_ev is not None:
                        wl.append(pb.free_ev)
                    for fc in range(NFC):
                        e = k.op("T", lambda t, c=c, fc=fc, pb=pb, buf=wob[bi]:
                                 t.matmul(
                            pb.ap[:], uT[:, fc, 128*c:128*(c+1)],
                            buf[:, fc, :], start=(fc == 0), stop=(fc == NFC - 1)),
                            tag=(fc == NFC - 1), waits=(wl if fc == 0 else ()))
                        if e:
                            evt = e
                    k.waits("V", [evt, ev_zn])
                    ev_xblk = k.op("V", lambda v, c=c, og=og, pb=pb:
                                   v.tensor_tensor(
                        xblk[:, c, 512*og:512*(og+1)], pb.ap[:], znb5[:],
                        ALU.add), tag=True)
                    pb.free_ev = ev_xblk
                last_o[bi] = evt
                if og + 2 < 4:
                    ev_wo[bi] = wblk_dma(2, og + 2, wob[bi], [last_o[bi]])
            k.barrier()

            ev_s4, _ = ln_stats(lambda c: xblk[:, c, :], ev_xblk,
                                mu8, rsig8, nmr8, extra_router=False)
            ev_h2 = None
            lastT4 = None
            for c in range(NCH):
                ev1 = k.op("C", lambda s, c=c: s.activation(
                    zn4[:], xblk[:, c, :], AF.Identity,
                    bias=nmr8[:, c:c+1], scale=rsig8[:, c:c+1]),
                    tag=True, waits=[ev_s4] + ([lastT4] if lastT4 else []))
                for fc in range(NFC):
                    pb = pbf[fc % 2]
                    wl = [ev1]
                    if pb.free_ev is not None:
                        wl.append(pb.free_ev)
                    evt = k.op("T", lambda t, fc=fc, pb=pb: t.transpose(
                        pb.ap[:, 0:128], zn4[:, 128*fc:128*(fc+1)], ident_b[:]),
                        tag=True, waits=wl)
                    lastT4 = evt
                    k.wait("V", evt)
                    ev_h2 = k.op("V", lambda v, c=c, fc=fc, pb=pb: v.tensor_scalar(
                        h2T[:, fc, 128*c:128*(c+1)], pb.ap[:, 0:128],
                        n2w[:, fc:fc+1], n2b[:, fc:fc+1], ALU.mult, ALU.add),
                        tag=True)
                    pb.free_ev = ev_h2
            k.barrier()

            a1r = aw1.rearrange("(fc p) h -> p fc h", p=128)
            a2r = aw2.rearrange("(fc p) h -> p fc h", p=128)
            for sb2 in range(2):
                if sb2 > 0:
                    k.barrier()
                ev_aw = {}
                use_aw = {}

                def aw_dma2(i, ev_aw=ev_aw, use_aw=use_aw):
                    h0 = 128 * i
                    hsz = HSZ[i]
                    wl = [use_aw[i - 2]] if (i - 2) in use_aw else []
                    ev = None
                    for q in range(4):
                        k.dma("G", awb[i % 2][:, 4*q:4*(q+1), 0:hsz],
                              a1r[:, 4*q:4*(q+1), h0:h0+hsz],
                              waits=(wl if q == 0 else ()))
                    for q in range(4):
                        ev = k.dma("G", aw2b[i % 2][:, 4*q:4*(q+1), 0:hsz],
                                   a2r[:, 4*q:4*(q+1), h0:h0+hsz])
                    ev_aw[i] = ev
                for i in range(2):
                    aw_dma2(i)
                ev_u2 = None
                for ha in range(NHA):
                    hsz = HSZ[ha]
                    bi = ha % 2
                    pa, pb2 = p512[0], p512[1]
                    wl = [ev_aw[ha], ev_h2]
                    if ev_u2:
                        wl.append(ev_u2)
                    for b_ in (pa, pb2):
                        if b_.free_ev is not None:
                            wl.append(b_.free_ev)
                    evta = None
                    for fc in range(NFC):
                        e = k.op("T", lambda t, fc=fc, hsz=hsz, sb2=sb2, pa=pa,
                                 bi=bi: t.matmul(
                            pa.ap[0:hsz, :], awb[bi][:, fc, 0:hsz],
                            h2T[:, fc, 512*sb2:512*(sb2+1)],
                            start=(fc == 0), stop=(fc == NFC - 1)),
                            tag=(fc == NFC - 1), waits=(wl if fc == 0 else ()))
                        if e:
                            evta = e
                    evtb = None
                    for fc in range(NFC):
                        e = k.op("T", lambda t, fc=fc, hsz=hsz, sb2=sb2, pb2=pb2,
                                 bi=bi: t.matmul(
                            pb2.ap[0:hsz, :], aw2b[bi][:, fc, 0:hsz],
                            h2T[:, fc, 512*sb2:512*(sb2+1)],
                            start=(fc == 0), stop=(fc == NFC - 1)),
                            tag=(fc == NFC - 1), waits=())
                        if e:
                            evtb = e
                    use_aw[ha] = evtb
                    if ha + 2 < NHA:
                        aw_dma2(ha + 2)
                    k.wait("C", evta)
                    eva = k.op("C", lambda s, hsz=hsz, pa=pa: s.activation(
                        aTs[0:hsz, :], pa.ap[0:hsz, :], AF.Silu), tag=True)
                    pa.free_ev = eva
                    k.waits("V", [eva, evtb])
                    ev_u2 = k.op("V", lambda v, ha=ha, hsz=hsz, pb2=pb2:
                                 v.tensor_tensor(
                        u2[0:hsz, ha, :], aTs[0:hsz, :], pb2.ap[0:hsz, :],
                        ALU.mult), tag=True)
                    pb2.free_ev = ev_u2
                ev_a3 = {}
                use_a3 = {}

                def a3_dma(j, ev_a3=ev_a3, use_a3=use_a3):
                    og, ha = j // NHA, j % NHA
                    hsz = HSZ[ha]
                    wl = [use_a3[j - 2]] if (j - 2) in use_a3 else []
                    ev_a3[j] = k.dma(
                        "G", aw3b[j % 2][0:hsz, :],
                        aw3[128*ha:128*ha+hsz, 512*og:512*(og+1)], waits=wl)
                for j in range(2):
                    a3_dma(j)
                for og in range(4):
                    pbs4 = [p512[2], p512[3], p512[4], p512[5]]
                    wl0 = [ev_u2]
                    for b_ in pbs4:
                        if b_.free_ev is not None:
                            wl0.append(b_.free_ev)
                    evt = None
                    for ha in range(NHA):
                        j = og * NHA + ha
                        hsz = HSZ[ha]
                        for ss in range(4):
                            wl2 = [ev_a3[j]] + (wl0 if ha == 0 else [])
                            e = k.op("T", lambda t, ha=ha, ss=ss, hsz=hsz, j=j,
                                     pb=pbs4[ss]: t.matmul(
                                pb.ap[:], u2[0:hsz, ha, 128*ss:128*(ss+1)],
                                aw3b[j % 2][0:hsz, :],
                                start=(ha == 0), stop=(ha == NHA - 1)),
                                tag=(ss == 3), waits=(wl2 if ss == 0 else ()))
                            if e:
                                evt = e
                        use_a3[j] = evt
                        if j + 2 < 4 * NHA:
                            a3_dma(j + 2)
                    for ss in range(4):
                        c = 4 * sb2 + ss
                        k.wait("V", evt)
                        ev = k.op("V", lambda v, c=c, og=og, pb=pbs4[ss]:
                                  v.tensor_tensor(
                            xblk[:, c, 512*og:512*(og+1)],
                            xblk[:, c, 512*og:512*(og+1)], pb.ap[:], ALU.add),
                            tag=True)
                        pbs4[ss].free_ev = ev
            k.barrier()

            ev_fin = None
            fins = []
            for c in range(NCH):
                ev_xi = k.dma("S", xbuf[:], x_page[:, c, :],
                              waits=([ev_fin] if ev_fin else []))
                k.waits("V", [ev_xi])
                ev_fin = k.op("V", lambda v, c=c: v.tensor_tensor(
                    xs_t[:, c, :], xblk[:, c, :], xbuf[:], ALU.add), tag=True)
                fins.append(ev_fin)
            return fins

        ev = ev_x
        if "ffn1" in parts:
            evs1 = emit_ffn(0, xs_t, ev)
            ev = evs1[-1]
            if "attn" in parts:
                pre_s2, _ = ln_stats(lambda c: xs_t[:, c, :],
                                     lambda c, evs1=evs1: [evs1[c]],
                                     mu2, rsig2, nmr2, extra_router=False)
            k.barrier()
        if "attn" in parts:
            fins = emit_attn(ev, pre_s2=pre_s2 if "ffn1" in parts else None)
            ev = fins[-1]
            if "ffn3" in parts:
                scrF3 = at("scrF3", 172 * KB, [128, D], F32)
                pre3 = ln_stats(lambda c: xs_t[:, c, :],
                                lambda c, fins=fins: [fins[c]],
                                mu8, rsig8, nmr8, extra_router=True,
                                router_scr=scrF3)
            k.barrier()
        if "ffn3" in parts:
            ev = emit_ffn(1, xs_t, ev,
                          pre=(pre3 if "attn" in parts else None))
            ev = ev[-1]
            k.barrier()

        k.barrier()
        dbg = ctx.enter_context(nc.sbuf_tensor("dbg", [128, 96], F32))
        for i, src in enumerate((mu8, rsig8, r08, w_own, ge8, sel8, pfx, ws8)):
            k.op("V", lambda v, i=i, src=src: v.tensor_copy(
                dbg[:, 8*i:8*(i+1)], src[:]))
        ev = k.op("V", lambda v: v.memset(dbg[:, 0:0+1], 0.0), tag=True)
        k.wait("V", ev)
        evd = k.dma("S", o_dbg, dbg[:], waits=[ev])
        evo = k.dma("S", o_out.rearrange("(c p) f -> p c f", p=128), xs_t)
        k.wait("S", evo)
        k.wait("S", evd)

        with nc.Block() as block:
            k.replay(block)
    return nc


def make_in_maps(inputs):
    x = np.ascontiguousarray(inputs["L0_attn_x"], dtype=np.float32)
    gam = gamma_vec()
    dh_scale = DH ** -0.5
    shared = {
        "wq": np.ascontiguousarray(inputs["L0_attn_wq"], np.float32),
        "wk": np.ascontiguousarray(inputs["L0_attn_wk"], np.float32),
        "wv": np.ascontiguousarray(inputs["L0_attn_wv"], np.float32),
        "wg": np.ascontiguousarray(inputs["L0_attn_wg"], np.float32),
        "wo": np.ascontiguousarray(inputs["L0_attn_wo"], np.float32),
        "aw1": np.ascontiguousarray(inputs["L0_attn_w1"], np.float32),
        "aw2": np.ascontiguousarray(inputs["L0_attn_w2"], np.float32),
        "aw3": np.ascontiguousarray(inputs["L0_attn_w3"], np.float32),
        "W1": np.ascontiguousarray(inputs["L0_ffn_W1"], np.float32),
        "W2": np.ascontiguousarray(inputs["L0_ffn_W2"], np.float32),
        "b1_i": np.ascontiguousarray(
            np.asarray(inputs["L0_ffn_b1"], np.float32).reshape(64, 128).T),
        "b2_i": np.asarray(inputs["L0_ffn_b2"], np.float32).reshape(1, D),
        "wr_b_i": np.ascontiguousarray(np.broadcast_to(
            np.asarray(inputs["L0_ffn_W_r"], np.float32).reshape(1, D), (128, D))),
        "br_i": np.ascontiguousarray(np.broadcast_to(
            np.asarray(inputs["L0_ffn_b_r"], np.float32).reshape(1, 1), (128, 1))),
        "n1w_i": np.ascontiguousarray(
            np.asarray(inputs["L0_attn_norm1_w"], np.float32).reshape(16, 128).T),
        "n1b_i": np.ascontiguousarray(
            np.asarray(inputs["L0_attn_norm1_b"], np.float32).reshape(16, 128).T),
        "n2w_i": np.ascontiguousarray(
            np.asarray(inputs["L0_attn_norm2_w"], np.float32).reshape(16, 128).T),
        "n2b_i": np.ascontiguousarray(
            np.asarray(inputs["L0_attn_norm2_b"], np.float32).reshape(16, 128).T),
    }
    in_maps = []
    for c in range(8):
        b, g = c // 2, c % 2
        off = g * T
        s = off + np.arange(T, dtype=np.float64)
        mq = np.empty((NH, 128, T), np.float32)
        mk = np.empty((NH, 128, T), np.float32)
        for h in range(NH):
            mq[h] = np.float32(gam[h] ** s)[None, :]
            mk[h] = np.float32(gam[h] ** (-s) * dh_scale)[None, :]
        flag = np.zeros((128, 2), np.float32)
        if g == 1:
            flag[:, 0] = 1.0
        m = dict(shared)
        m["x_in"] = np.ascontiguousarray(x[b, off:off + T])
        m["mq_i"] = mq
        m["mk_i"] = mk
        m["flag_i"] = flag
        in_maps.append(m)
    return in_maps


_PARTS = ("ffn1", "attn", "ffn3")


def kernel(**inputs):
    in_maps = make_in_maps(inputs)
    nc = build_program(_PARTS)
    res = run_bass_kernel_spmd(nc, in_maps, core_ids=list(range(8)))
    out = np.empty((4, 2048, D), np.float32)
    for c in range(8):
        b, g = c // 2, c % 2
        out[b, g * T:(g + 1) * T] = res.results[c]["o"]
    return out
